# version 01
# baseline (speedup 1.0000x reference)
import math
from contextlib import ExitStack

import numpy as np
import ml_dtypes

import concourse.bass as bass
import concourse.mybir as mybir
from concourse.bass_utils import run_bass_kernel_spmd

F32 = mybir.dt.float32
BF16 = mybir.dt.bfloat16
AF = mybir.ActivationFunctionType
ALU = mybir.AluOpType
AX = mybir.AxisListType.X
bfdt = ml_dtypes.bfloat16

NB, S, D = 2, 4096, 4096
NH, HD = 32, 128
INTER, RANK = 11008, 1024
TOK = 1024
NPT = 32
NI = 86
SCALE = 1.0 / math.sqrt(HD)
EPS = 1e-5
NEG = -1e30
SLOPES = [2.0 ** (-(i + 1) / 4.0) for i in range(NH)]


def _scidx(i, c):
    return i * (i + 1) // 2 + c


N_SC = 36



def _pack_weights(Wq, Wkv, Wo, U1, V1, U2, V2, b1, b2):
    p = {}
    p["u1p"] = np.ascontiguousarray(
        U1.reshape(32, 128, 8, 128).transpose(2, 1, 0, 3).reshape(8, 128, 4096))
    v1t = V1.reshape(8, 128, 172, 128).transpose(2, 1, 0, 3).reshape(172, 128, 1024)
    u2t = U2.reshape(86, 128, 8, 128).reshape(86, 128, 1024)
    p["v1u2p"] = np.ascontiguousarray(
        np.concatenate([v1t[:86], v1t[86:], u2t], axis=2))
    p["v2p"] = np.ascontiguousarray(
        V2.reshape(8, 128, 32, 128).transpose(2, 1, 0, 3).reshape(32, 128, 1024))
    p["wqp"] = np.ascontiguousarray(
        Wq.reshape(32, 128, 16, 256).transpose(2, 1, 0, 3).reshape(16, 128, 8192))
    p["wop"] = np.ascontiguousarray(
        Wo.reshape(32, 128, 16, 256).transpose(2, 1, 0, 3).reshape(16, 128, 8192))
    p["wkvp"] = np.ascontiguousarray(
        Wkv.reshape(32, 128, 256).transpose(1, 0, 2).reshape(128, 8192))
    p["b1p"] = np.ascontiguousarray(b1.reshape(172, 128).T)
    p["b2p"] = np.ascontiguousarray(b2.reshape(32, 128).T)
    return p


def _consts_for_core(j):
    c = {}
    dq = np.arange(128, dtype=np.float32)[:, None]
    dk = np.arange(512, dtype=np.float32)[None, :]
    c["r0"] = np.ascontiguousarray((dk - dq).astype(np.float32))
    c["mask1"] = np.where(dk - dq <= 128.0 * j, 0.0, NEG).astype(np.float32)
    ot = np.zeros((NH, N_SC), dtype=np.float32)
    for h in range(NH):
        for i in range(8):
            for cc in range(i + 1):
                ot[h, _scidx(i, cc)] = SLOPES[h] * (512.0 * cc - 512.0 * i - 128.0 * j)
    c["otab"] = np.ascontiguousarray(
        np.broadcast_to(ot.reshape(1, NH * N_SC), (128, NH * N_SC)).astype(np.float32))
    c["idn_bf"] = np.eye(128, dtype=bfdt)
    c["idn_f32"] = np.eye(128, dtype=np.float32)
    c["ones_bf"] = np.ones((128, 1), dtype=bfdt)
    c["ones_f32r"] = np.ones((1, 128), dtype=np.float32)
    return c


def _core_token_rows(j):
    idx = []
    for i in range(8):
        t0 = 128 * (4 * i + j)
        idx.extend(range(t0, t0 + 128))
    return np.array(idx, dtype=np.int64)




class Sems:
    def __init__(self, nc, es, names):
        self.h = {n: es.enter_context(nc.semaphore(n)) for n in names}
        self.n = {n: 0 for n in names}

    def inc(self, instr, name, k=1):
        instr.then_inc(self.h[name], k)
        self.n[name] += k
        return self.n[name]

    def __getitem__(self, name):
        return self.h[name]


def _common_bufs(nc, es, pfx=""):
    sb = lambda n, sh, dt: es.enter_context(nc.sbuf_tensor(pfx + n, sh, dt))
    bufs = {
        "xst": [sb(f"xst{i}", [128, 1024], F32) for i in range(2)],
        "xbf": [sb(f"xbf{i}", [128, 1024], BF16) for i in range(2)],
        "xsq": [sb(f"xsq{i}", [128, 1024], BF16) for i in range(2)],
        "row": sb("row", [1, 4096], F32),
        "mb_sb": sb("mb", [128, 1024], F32),
        "ib_sb": sb("ib", [128, 1024], F32),
        "u1st": [sb(f"u1st{i}", [128, 512], F32) for i in range(2)],
        "u1bf": [sb(f"u1bf{i}", [128, 512], BF16) for i in range(2)],
        "wst": [sb(f"wst{i}", [128, 1536], F32) for i in range(2)],
        "wbf": [sb(f"wbf{i}", [128, 3072], BF16) for i in range(2)],
        "v2st": [sb(f"v2st{i}", [128, 512], F32) for i in range(2)],
        "v2bf": [sb(f"v2bf{i}", [128, 1024], BF16) for i in range(2)],
        "sg_sb": sb("sg", [128, 1024], F32),
        "ub_sb": sb("ub", [128, 1024], F32),
        "act_bf": [sb(f"act{i}", [128, 1024], BF16) for i in range(2)],
        "b1p": sb("b1ps", [128, 172], F32),
        "b2h": sb("b2h", [128, 32], F32),
        "ones_bf": sb("onesbf", [128, 1], BF16),
        "ones_f32r": sb("onesf", [1, 128], F32),
    }
    bufs["xst2"] = bufs["xst"]
    bufs["t_sb"] = [bufs["sg_sb"], bufs["ub_sb"]]
    return bufs


LN_SEMS = ["lnld0", "lnld1", "lncast", "lnmm", "lnfin", "lnap",
           "lnld20", "lnld21"]
FFN_SEMS = ["fu1ld0", "fu1ld1", "fu1c", "fu1u", "fpev",
            "fwld0", "fwld1", "fwc",
            "fhmm", "fhdr", "fact", "fzmm", "fzd",
            "fv2ld0", "fv2ld1", "fv2c",
            "fomm", "ft", "fxld0", "fxld1", "fxadd", "fst0", "fst1", "fzcast"]


_LN_N = [0]


def _emit_ln(nc, sm, bufs, src_dram, y_sb, pre=None):
    _LN_N[0] += 1
    tag = f"_{_LN_N[0]}"
    with ExitStack() as pse:
        psf = lambda n, sh: pse.enter_context(nc.psum_tensor(n + tag, sh, F32))
        sx = [psf(f"lnsx{i}", [1, 512]) for i in range(2)]
        sq = [psf(f"lnsq{i}", [1, 512]) for i in range(2)]
        bc = [psf(f"lnbc{i}", [128, 512]) for i in range(4)]
        xst, xbf, xsq = bufs["xst"], bufs["xbf"], bufs["xsq"]
        row, mb_sb, ib_sb = bufs["row"], bufs["mb_sb"], bufs["ib_sb"]
        ones_bf, ones_f32r = bufs["ones_bf"], bufs["ones_f32r"]
        b_ld = [sm.n["lnld0"], sm.n["lnld1"]]
        b_ld2 = [sm.n["lnld20"], sm.n["lnld21"]]
        b_cast = sm.n["lncast"]; b_mm = sm.n["lnmm"]
        b_fin = sm.n["lnfin"]; b_ap = sm.n["lnap"]

        with nc.Block() as blk:
            @blk.gpsimd
            def _(g):
                for pt in range(NPT):
                    if pt == 0 and pre is not None:
                        for pn, pv in pre:
                            g.wait_ge(sm[pn], pv)
                    if pt >= 2:
                        g.wait_ge(sm["lncast"], b_cast + pt - 1)
                    sm.inc(g.dma_start(out=xst[pt % 2][:, :], in_=src_dram[pt, :, :]),
                           f"lnld{pt % 2}", 16)
                for pt in range(NPT):
                    if pt < 2:
                        g.wait_ge(sm["lncast"], b_cast + 31 + pt)
                    else:
                        g.wait_ge(sm["lnap"], b_ap + 2 * (pt - 1))
                    sm.inc(g.dma_start(out=xst[pt % 2][:, :], in_=src_dram[pt, :, :]),
                           f"lnld2{pt % 2}", 16)

            @blk.scalar
            def _(s):
                for pt in range(NPT):
                    s.wait_ge(sm[f"lnld{pt % 2}"], b_ld[pt % 2] + 16 * (pt // 2 + 1))
                    if pt >= 2:
                        s.wait_ge(sm["lnmm"], b_mm + pt - 1)
                    s.activation(xbf[pt % 2][:, :], xst[pt % 2][:, :], AF.Copy)
                    sm.inc(s.activation(xsq[pt % 2][:, :], xst[pt % 2][:, :],
                                        AF.Square), "lncast")
                s.wait_ge(sm["lnmm"], b_mm + NPT)
                for h2 in range(2):
                    last = s.activation(row[:, h2 * 512:(h2 + 1) * 512], sx[h2][:, :],
                                        AF.Copy, scale=1.0 / 4096.0)
                sm.inc(last, "lnfin")
                s.wait_ge(sm["lnfin"], b_fin + 4)
                sm.inc(s.activation(row[:, 3072:4096], row[:, 2048:3072], AF.Sqrt),
                       "lnfin")

            @blk.tensor
            def _(t):
                for pt in range(NPT):
                    t.wait_ge(sm["lncast"], b_cast + pt + 1)
                    for h2 in range(2):
                        c0 = h2 * 512
                        t.matmul(sx[h2][:, :], ones_bf[:, :], xbf[pt % 2][:, c0:c0 + 512],
                                 start=(pt == 0), stop=(pt == NPT - 1))
                        last = t.matmul(sq[h2][:, :], ones_bf[:, :],
                                        xsq[pt % 2][:, c0:c0 + 512],
                                        start=(pt == 0), stop=(pt == NPT - 1))
                    sm.inc(last, "lnmm")
                t.wait_ge(sm["lnfin"], b_fin + 6)
                for h2 in range(2):
                    t.matmul(bc[h2][:, :], ones_f32r[:, :],
                             row[:, h2 * 512:(h2 + 1) * 512], start=True, stop=True)
                    last = t.matmul(bc[2 + h2][:, :], ones_f32r[:, :],
                                    row[:, 1024 + h2 * 512:1024 + (h2 + 1) * 512],
                                    start=True, stop=True)
                sm.inc(last, "lnmm")

            @blk.vector
            def _(v):
                v.wait_ge(sm["lnfin"], b_fin + 1)
                sm.inc(v.tensor_mul(row[:, 1024:2048], row[:, 0:1024], row[:, 0:1024]),
                       "lnfin")
                v.wait_ge(sm["lnfin"], b_fin + 2)
                for h2 in range(2):
                    last = v.scalar_tensor_tensor(
                        row[:, 2048 + h2 * 512:2048 + (h2 + 1) * 512], sq[h2][:, :],
                        1.0 / 4096.0, row[:, 1024 + h2 * 512:1024 + (h2 + 1) * 512],
                        ALU.mult, ALU.subtract)
                sm.inc(last, "lnfin")
                v.wait_ge(sm["lnfin"], b_fin + 3)
                sm.inc(v.tensor_scalar(row[:, 2048:3072], row[:, 2048:3072], EPS,
                                       None, ALU.add), "lnfin")
                v.wait_ge(sm["lnfin"], b_fin + 5)
                sm.inc(v.reciprocal(row[:, 1024:2048], row[:, 3072:4096]), "lnfin")

            @blk.scalar
            def _(s):
                s.wait_ge(sm["lnmm"], b_mm + NPT + 1)
                for h2 in range(2):
                    s.activation(mb_sb[:, h2 * 512:(h2 + 1) * 512], bc[h2][:, :], AF.Copy)
                    last = s.activation(ib_sb[:, h2 * 512:(h2 + 1) * 512],
                                        bc[2 + h2][:, :], AF.Copy)
                sm.inc(last, "lnfin")

            @blk.vector
            def _(v):
                v.wait_ge(sm["lnfin"], b_fin + 7)
                for pt in range(NPT):
                    v.wait_ge(sm[f"lnld2{pt % 2}"], b_ld2[pt % 2] + 16 * (pt // 2 + 1))
                    sm.inc(v.tensor_sub(xst[pt % 2][:, :], xst[pt % 2][:, :],
                                        mb_sb[:, :]), "lnap")
                    v.wait_ge(sm["lnap"], b_ap + 2 * pt + 1)
                    sm.inc(v.tensor_mul(y_sb[:, pt * 1024:(pt + 1) * 1024],
                                        xst[pt % 2][:, :], ib_sb[:, :]), "lnap")


def _emit_ffn(nc, sm, bufs, y_sb, p_sb, zacc, xsrc, xdst, wd, is_final):
    with ExitStack() as pse:
        psf = lambda n, sh: pse.enter_context(nc.psum_tensor(n, sh, F32))
        q4 = [psf(f"fq{i}", [128, 512]) for i in range(4)]
        zp = [psf(f"fzp{i}", [128, 1024]) for i in range(2)]
        u1st, u1bf = bufs["u1st"], bufs["u1bf"]
        wst, wbf = bufs["wst"], bufs["wbf"]
        v2st, v2bf = bufs["v2st"], bufs["v2bf"]
        sg_sb, ub_sb = bufs["sg_sb"], bufs["ub_sb"]
        act_bf, t_sb, xst2 = bufs["act_bf"], bufs["t_sb"], bufs["xst2"]
        b1p, b2h = bufs["b1p"], bufs["b2h"]
        b = {k: sm.n[k] for k in FFN_SEMS}

        with nc.Block() as blk:
            @blk.sync
            def _(s):
                for pm in range(8):
                    for kc in range(8):
                        n = pm * 8 + kc
                        if n >= 2:
                            s.wait_ge(sm["fu1c"], b["fu1c"] + n - 1)
                        sm.inc(s.dma_start(out=u1st[n % 2][:, :],
                                           in_=wd["u1p"][pm, :, kc * 512:(kc + 1) * 512]),
                               f"fu1ld{n % 2}", 16)
                for i in range(NI):
                    for hf in range(2):
                        n = 2 * i + hf
                        if n >= 2:
                            s.wait_ge(sm["fwc"], b["fwc"] + n - 1)
                        sm.inc(s.dma_start(
                            out=wst[n % 2][:, :],
                            in_=wd["v1u2p"][i, :, hf * 1536:(hf + 1) * 1536]),
                               f"fwld{n % 2}", 16)
                for om in range(32):
                    for hf in range(2):
                        n = 2 * om + hf
                        if n >= 2:
                            s.wait_ge(sm["fv2c"], b["fv2c"] + n - 1)
                        sm.inc(s.dma_start(
                            out=v2st[n % 2][:, :],
                            in_=wd["v2p"][om, :, hf * 512:(hf + 1) * 512]),
                               f"fv2ld{n % 2}", 16)

            @blk.gpsimd
            def _(g):
                for om in range(32):
                    if om >= 2:
                        g.wait_ge(sm[f"fst{om % 2}"],
                                  b[f"fst{om % 2}"] + 16 * (om // 2))
                    sm.inc(g.dma_start(out=xst2[om % 2][:, :], in_=xsrc[om, :, :]),
                           f"fxld{om % 2}", 16)
                    g.wait_ge(sm["fxadd"], b["fxadd"] + om + 1)
                    sm.inc(g.dma_start(out=xdst[om, :, :], in_=xst2[om % 2][:, :]),
                           f"fst{om % 2}", 16)
                if is_final:
                    g.wait_ge(sm["fst0"], b["fst0"] + 16 * 16)
                    g.wait_ge(sm["fst1"], b["fst1"] + 16 * 16)

            @blk.scalar
            def _(s):
                def u1cast(n):
                    s.wait_ge(sm[f"fu1ld{n % 2}"],
                              b[f"fu1ld{n % 2}"] + 16 * (n // 2 + 1))
                    if n >= 2:
                        s.wait_ge(sm["fu1u"], b["fu1u"] + n - 1)
                    sm.inc(s.activation(u1bf[n % 2][:, :], u1st[n % 2][:, :],
                                        AF.Copy), "fu1c")

                u1cast(0)
                u1cast(1)
                for pm in range(8):
                    for kc in range(8):
                        n = pm * 8 + kc
                        if n + 2 < 64:
                            u1cast(n + 2)
                    if pm >= 1:
                        s.wait_ge(sm["fu1u"], b["fu1u"] + 8 * pm)
                        for h2 in range(2):
                            c0 = (pm - 1) * 1024 + h2 * 512
                            last = s.activation(p_sb[:, c0:c0 + 512],
                                                q4[2 * ((pm - 1) % 2) + h2][:, :],
                                                AF.Copy)
                        sm.inc(last, "fpev")
                s.wait_ge(sm["fu1u"], b["fu1u"] + 64)
                for h2 in range(2):
                    c0 = 7 * 1024 + h2 * 512
                    last = s.activation(p_sb[:, c0:c0 + 512], q4[2 + h2][:, :], AF.Copy)
                sm.inc(last, "fpev")

                def wcast(n):
                    i2, hf = n // 2, n % 2
                    s.wait_ge(sm[f"fwld{n % 2}"],
                              b[f"fwld{n % 2}"] + 16 * (n // 2 + 1))
                    if i2 >= 2 and hf == 0:
                        s.wait_ge(sm["fzmm"], b["fzmm"] + 8 * (i2 - 1))
                    sm.inc(s.activation(wbf[i2 % 2][:, hf * 1536:(hf + 1) * 1536],
                                        wst[n % 2][:, :], AF.Copy), "fwc")

                for n in range(4):
                    wcast(n)
                for i in range(NI):
                    s.wait_ge(sm["fhmm"], b["fhmm"] + i + 1)
                    if i >= 1:
                        s.wait_ge(sm["fact"], b["fact"] + i)
                    for h2 in range(2):
                        c0 = h2 * 512
                        s.activation(sg_sb[:, c0:c0 + 512], q4[h2][:, :], AF.Silu,
                                     bias=b1p[:, i:i + 1])
                        last = s.activation(ub_sb[:, c0:c0 + 512], q4[2 + h2][:, :],
                                            AF.Identity, bias=b1p[:, 86 + i:87 + i])
                    sm.inc(last, "fhdr")
                    for hf in range(2):
                        n = 2 * (i + 2) + hf
                        if n < 2 * NI:
                            wcast(n)
                s.wait_ge(sm["fzd"], b["fzd"] + NI * 8)
                sm.inc(s.activation(p_sb[:, :], zacc[:, :], AF.Copy), "fzcast")

                def v2cast(n):
                    om2, hf = n // 2, n % 2
                    s.wait_ge(sm[f"fv2ld{n % 2}"],
                              b[f"fv2ld{n % 2}"] + 16 * (n // 2 + 1))
                    if om2 >= 2 and hf == 0:
                        s.wait_ge(sm["fomm"], b["fomm"] + om2 - 1)
                    sm.inc(s.activation(v2bf[om2 % 2][:, hf * 512:(hf + 1) * 512],
                                        v2st[n % 2][:, :], AF.Copy), "fv2c")

                for n in range(4):
                    v2cast(n)
                for om in range(32):
                    s.wait_ge(sm["fomm"], b["fomm"] + om + 1)
                    if om >= 2:
                        s.wait_ge(sm["fxadd"], b["fxadd"] + om - 1)
                    for h2 in range(2):
                        last = s.activation(t_sb[om % 2][:, h2 * 512:(h2 + 1) * 512],
                                            q4[2 * (om % 2) + h2][:, :], AF.Identity,
                                            scale=0.5, bias=b2h[:, om:om + 1])
                    sm.inc(last, "ft")
                    for hf in range(2):
                        n = 2 * (om + 2) + hf
                        if n < 64:
                            v2cast(n)

            @blk.tensor
            def _(t):
                for pm in range(8):
                    if pm >= 2:
                        t.wait_ge(sm["fpev"], b["fpev"] + pm - 1)
                    for kc in range(8):
                        n = pm * 8 + kc
                        t.wait_ge(sm["fu1c"], b["fu1c"] + n + 1)
                        for j in range(4):
                            k = kc * 4 + j
                            for h2 in range(2):
                                last = t.matmul(q4[2 * (pm % 2) + h2][:, :],
                                                u1bf[n % 2][:, j * 128:(j + 1) * 128],
                                                y_sb[:, k * 1024 + h2 * 512:k * 1024 + (h2 + 1) * 512],
                                                start=(k == 0), stop=(k == 31))
                        sm.inc(last, "fu1u")
                for i in range(NI + 1):
                    if i < NI:
                        t.wait_ge(sm["fwc"], b["fwc"] + 2 * (i + 1))
                        t.wait_ge(sm["fhdr"], b["fhdr"] + i)
                        if i == 0:
                            t.wait_ge(sm["fpev"], b["fpev"] + 8)
                        for part in range(2):
                            for k in range(8):
                                for h2 in range(2):
                                    cw = part * 1024 + k * 128
                                    last = t.matmul(
                                        q4[2 * part + h2][:, :],
                                        wbf[i % 2][:, cw:cw + 128],
                                        p_sb[:, k * 1024 + h2 * 512:k * 1024 + (h2 + 1) * 512],
                                        start=(k == 0), stop=(k == 7))
                        sm.inc(last, "fhmm")
                    if i >= 1:
                        ii = i - 1
                        t.wait_ge(sm["fact"], b["fact"] + ii + 1)
                        for zm in range(8):
                            g = ii * 8 + zm
                            if g >= 2:
                                t.wait_ge(sm["fzd"], b["fzd"] + g - 1)
                            for h2 in range(2):
                                last = t.matmul(
                                    zp[g % 2][:, h2 * 512:(h2 + 1) * 512],
                                    wbf[ii % 2][:, 2048 + zm * 128:2048 + (zm + 1) * 128],
                                    act_bf[ii % 2][:, h2 * 512:(h2 + 1) * 512],
                                    start=True, stop=True)
                            sm.inc(last, "fzmm")
                for om in range(32):
                    t.wait_ge(sm["fv2c"], b["fv2c"] + 2 * (om + 1))
                    if om == 0:
                        t.wait_ge(sm["fzcast"], b["fzcast"] + 1)
                        t.wait_ge(sm["fhdr"], b["fhdr"] + NI)
                    if om >= 2:
                        t.wait_ge(sm["ft"], b["ft"] + om - 1)
                    for k in range(8):
                        for h2 in range(2):
                            last = t.matmul(q4[2 * (om % 2) + h2][:, :],
                                            v2bf[om % 2][:, k * 128:(k + 1) * 128],
                                            p_sb[:, k * 1024 + h2 * 512:k * 1024 + (h2 + 1) * 512],
                                            start=(k == 0), stop=(k == 7))
                    sm.inc(last, "fomm")

            @blk.vector
            def _(v):
                for i in range(NI):
                    if i >= 1:
                        ii = i - 1
                        for zm in range(8):
                            g = ii * 8 + zm
                            v.wait_ge(sm["fzmm"], b["fzmm"] + g + 1)
                            dst = zacc[:, zm * 1024:(zm + 1) * 1024]
                            if ii == 0:
                                sm.inc(v.tensor_copy(dst, zp[g % 2][:, :]), "fzd")
                            else:
                                sm.inc(v.tensor_add(dst, dst, zp[g % 2][:, :]), "fzd")
                    v.wait_ge(sm["fhdr"], b["fhdr"] + i + 1)
                    if i >= 2:
                        v.wait_ge(sm["fzmm"], b["fzmm"] + 8 * (i - 1))
                    sm.inc(v.tensor_mul(act_bf[i % 2][:, :], sg_sb[:, :], ub_sb[:, :]),
                           "fact")
                ii = NI - 1
                for zm in range(8):
                    g = ii * 8 + zm
                    v.wait_ge(sm["fzmm"], b["fzmm"] + g + 1)
                    dst = zacc[:, zm * 1024:(zm + 1) * 1024]
                    sm.inc(v.tensor_add(dst, dst, zp[g % 2][:, :]), "fzd")
                for om in range(32):
                    v.wait_ge(sm["ft"], b["ft"] + om + 1)
                    v.wait_ge(sm[f"fxld{om % 2}"],
                              b[f"fxld{om % 2}"] + 16 * (om // 2 + 1))
                    sm.inc(v.tensor_add(xst2[om % 2][:, :], xst2[om % 2][:, :],
                                        t_sb[om % 2][:, :]), "fxadd")




def build_A():
    nc = bass.Bass("TRN2", target_bir_lowering=False)
    X = nc.dram_tensor("x", [NPT, 128, TOK], F32, kind="ExternalInput")
    U1P = nc.dram_tensor("u1p", [8, 128, 4096], F32, kind="ExternalInput")
    V1U2P = nc.dram_tensor("v1u2p", [86, 128, 3072], F32, kind="ExternalInput")
    V2P = nc.dram_tensor("v2p", [32, 128, 1024], F32, kind="ExternalInput")
    WKVP = nc.dram_tensor("wkvp", [128, 8192], F32, kind="ExternalInput")
    B1P = nc.dram_tensor("b1p", [128, 172], F32, kind="ExternalInput")
    B2P = nc.dram_tensor("b2p", [128, 32], F32, kind="ExternalInput")
    ONESBF = nc.dram_tensor("ones_bf", [128, 1], BF16, kind="ExternalInput")
    ONESF = nc.dram_tensor("ones_f32r", [1, 128], F32, kind="ExternalInput")
    X1 = nc.dram_tensor("x1", [NPT, 128, TOK], F32, kind="ExternalOutput")
    Y2 = nc.dram_tensor("y2", [NPT, 128, TOK], BF16, kind="ExternalOutput")
    KV = nc.dram_tensor("kv", [2, 128, TOK], BF16, kind="ExternalOutput")
    wd = {"u1p": U1P, "v1u2p": V1U2P, "v2p": V2P}

    with ExitStack() as es:
        sb = lambda n, sh, dt: es.enter_context(nc.sbuf_tensor(n, sh, dt))
        y_sb = sb("y", [128, NPT * 1024], BF16)
        p_sb = sb("p", [128, 8 * 1024], BF16)
        zacc = sb("zacc", [128, 8 * 1024], F32)
        bufs = _common_bufs(nc, es)
        kvw = bufs["u1st"]
        kvwb = bufs["u1bf"]
        kv_bf = bufs["act_bf"]
        sems = Sems(nc, es, ["init"] + LN_SEMS + FFN_SEMS +
                    ["kvld0", "kvld1", "kvc", "kvmm", "kvev", "yst"])

        with nc.Block() as blk:
            @blk.gpsimd
            def _(g):
                for dst, src in ((bufs["b1p"], B1P), (bufs["b2h"], B2P),
                                 (bufs["ones_bf"], ONESBF), (bufs["ones_f32r"], ONESF)):
                    sems.inc(g.dma_start(out=dst[:, :], in_=src[:, :]), "init", 16)

            @blk.vector
            def _(v):
                v.wait_ge(sems["init"], 64)
                v.tensor_scalar(bufs["b2h"][:, :], bufs["b2h"][:, :], 0.5, None,
                                ALU.mult)

        _emit_ln(nc, sems, bufs, X, y_sb)
        _emit_ffn(nc, sems, bufs, y_sb, p_sb, zacc, X, X1, wd, is_final=False)
        _emit_ln(nc, sems, bufs, X1, y_sb,
                 pre=[("fst0", sems.n["fst0"]), ("fst1", sems.n["fst1"])])

        with ExitStack() as pse:
            kvp = [pse.enter_context(nc.psum_tensor(f"kvp{i}", [128, 512], F32))
                   for i in range(4)]
            with nc.Block() as blk:
                @blk.gpsimd
                def _(g):
                    for pt in range(NPT):
                        sems.inc(g.dma_start(out=Y2[pt, :, :],
                                             in_=y_sb[:, pt * 1024:(pt + 1) * 1024]),
                                 "yst", 16)

                @blk.sync
                def _(s):
                    for k in range(32):
                        if k >= 2:
                            s.wait_ge(sems["kvc"], k - 1)
                        sems.inc(s.dma_start(out=kvw[k % 2][:, 0:256],
                                             in_=WKVP[:, k * 256:(k + 1) * 256]),
                                 f"kvld{k % 2}", 16)

                @blk.scalar
                def _(s):
                    for k in range(32):
                        s.wait_ge(sems[f"kvld{k % 2}"], 16 * (k // 2 + 1))
                        if k >= 2:
                            s.wait_ge(sems["kvmm"], k - 1)
                        sems.inc(s.activation(kvwb[k % 2][:, 0:256],
                                              kvw[k % 2][:, 0:256], AF.Copy), "kvc")
                    s.wait_ge(sems["kvmm"], 32)
                    for m2 in range(2):
                        for h2 in range(2):
                            last = s.activation(kv_bf[m2][:, h2 * 512:(h2 + 1) * 512],
                                                kvp[2 * m2 + h2][:, :], AF.Copy)
                    sems.inc(last, "kvev")

                @blk.tensor
                def _(t):
                    for k in range(32):
                        t.wait_ge(sems["kvc"], k + 1)
                        for m2 in range(2):
                            for h2 in range(2):
                                last = t.matmul(
                                    kvp[2 * m2 + h2][:, :],
                                    kvwb[k % 2][:, m2 * 128:(m2 + 1) * 128],
                                    y_sb[:, k * 1024 + h2 * 512:k * 1024 + (h2 + 1) * 512],
                                    start=(k == 0), stop=(k == 31))
                        sems.inc(last, "kvmm")

                @blk.gpsimd
                def _(g):
                    g.wait_ge(sems["kvev"], 1)
                    for m2 in range(2):
                        sems.inc(g.dma_start(out=KV[m2, :, :], in_=kv_bf[m2][:, :]),
                                 "yst", 16)
                    g.wait_ge(sems["yst"], 16 * (NPT + 2))
                    g.wait_ge(sems["fst0"], 16 * 16)
                    g.wait_ge(sems["fst1"], 16 * 16)
    return nc




def build_B():
    nc = bass.Bass("TRN2", target_bir_lowering=False)
    X1 = nc.dram_tensor("x1", [NPT, 128, TOK], F32, kind="ExternalInput")
    Y2D = nc.dram_tensor("y2", [NPT, 128, TOK], BF16, kind="ExternalInput")
    KVG = nc.dram_tensor("kvg", [4, 2, 128, TOK], BF16, kind="ExternalInput")
    WQP = nc.dram_tensor("wqp", [16, 128, 8192], F32, kind="ExternalInput")
    WOP = nc.dram_tensor("wop", [16, 128, 8192], F32, kind="ExternalInput")
    U1P = nc.dram_tensor("u1p", [8, 128, 4096], F32, kind="ExternalInput")
    V1U2P = nc.dram_tensor("v1u2p", [86, 128, 3072], F32, kind="ExternalInput")
    V2P = nc.dram_tensor("v2p", [32, 128, 1024], F32, kind="ExternalInput")
    B1P = nc.dram_tensor("b1p", [128, 172], F32, kind="ExternalInput")
    B2P = nc.dram_tensor("b2p", [128, 32], F32, kind="ExternalInput")
    R0D = nc.dram_tensor("r0", [128, 512], F32, kind="ExternalInput")
    MK1 = nc.dram_tensor("mask1", [128, 512], F32, kind="ExternalInput")
    OTAB = nc.dram_tensor("otab", [128, NH * N_SC], F32, kind="ExternalInput")
    IDNB = nc.dram_tensor("idn_bf", [128, 128], BF16, kind="ExternalInput")
    IDNF = nc.dram_tensor("idn_f32", [128, 128], F32, kind="ExternalInput")
    ONESBF = nc.dram_tensor("ones_bf", [128, 1], BF16, kind="ExternalInput")
    ONESF = nc.dram_tensor("ones_f32r", [1, 128], F32, kind="ExternalInput")
    OUT = nc.dram_tensor("out", [NPT, 128, TOK], F32, kind="ExternalOutput")
    XB = nc.dram_tensor("xbuf", [NPT, 128, TOK], F32, kind="Internal")
    wd = {"u1p": U1P, "v1u2p": V1U2P, "v2p": V2P}

    with ExitStack() as es0:
        sb0 = lambda n, sh, dt: es0.enter_context(nc.sbuf_tensor(n, sh, dt))
        b1p_sb = sb0("b1ps", [128, 172], F32)
        b2h_sb = sb0("b2h", [128, 32], F32)
        onesbf_sb = sb0("onesbf", [128, 1], BF16)
        onesf_sb = sb0("onesf", [1, 128], F32)
        sems = Sems(nc, es0, ["init"] + LN_SEMS + FFN_SEMS + [
            "vktr", "vkev", "knsq", "knmm", "knev",
            "qwld0", "qwld1", "qwc", "qwu", "qev",
            "qnsq", "qnmm", "qnev", "bch",
            "scmm", "scv", "sexp", "strc", "ptev", "spv",
            "nsum", "nrow1", "nrec", "nrb2", "nrbc", "ndone", "relw",
            "wwld0", "wwld1", "wwc", "wwu",
            "wxld0", "wxld1", "wxadd", "wst20", "wst21",
        ])

        with ExitStack() as esA:
            sbA = lambda n, sh, dt: esA.enter_context(nc.sbuf_tensor(n, sh, dt))
            r0_sb = sbA("r0s", [128, 512], F32)
            mk_sb = sbA("mk1s", [128, 512], F32)
            ot_sb = sbA("ots", [128, NH * N_SC], F32)
            ba_sb = sbA("bas", [128, NH * N_SC], F32)
            idnb_sb = sbA("idnb", [128, 128], BF16)
            idnf_sb = sbA("idnf", [128, 128], F32)
            relh_sb = sbA("relh", [128, 512], F32)
            rm_sb = sbA("relm", [128, 512], F32)
            sc2_sb = [sbA(f"sc2{i}", [128, 512], F32) for i in range(2)]
            pr_sb = [sbA(f"pr{i}", [128, 512], BF16) for i in range(4)]
            pt_sb = sbA("pts", [128, 32 * 128], BF16)
            sums_sb = sbA("sumss", [128, 2 * N_SC], F32)
            ssum_sb = sbA("ssum", [128, 1], F32)
            srow_sb = sbA("srow", [1, 128], F32)
            rb_sb = sbA("rbs", [128, 128], F32)
            qsq_sb = sbA("qsqs", [128, 1024], BF16)
            nrow_sb = sbA("nrow", [1, 1024], F32)
            bsc_sb = sbA("bsc", [1, 96], F32)
            bcol_sb = sbA("bcol", [128, 32], F32)
            wqs = [sbA(f"wqs{i}", [128, 256], F32) for i in range(2)]
            wqb = [sbA(f"wqb{i}", [128, 256], BF16) for i in range(2)]
            kt_sb = sbA("kt", [128, 4096], BF16)
            vk_sb = sbA("vk", [128, 4096], BF16)
            y_sb = sbA("y", [128, NPT * 1024], BF16)
            q_sb = sbA("q", [128, NPT * 1024], BF16)

            with ExitStack() as esV:
                vt_sb = esV.enter_context(nc.sbuf_tensor("vt", [128, 4096], BF16))
                tp = [esV.enter_context(nc.psum_tensor(f"vtp{i}", [128, 128], BF16))
                      for i in range(2)]
                knp = esV.enter_context(nc.psum_tensor("knp", [1, 512], F32))

                with nc.Block() as blk:
                    @blk.gpsimd
                    def _(g):
                        for dst, src in ((b1p_sb, B1P), (b2h_sb, B2P),
                                         (onesbf_sb, ONESBF), (onesf_sb, ONESF),
                                         (r0_sb, R0D), (mk_sb, MK1), (ot_sb, OTAB),
                                         (idnb_sb, IDNB), (idnf_sb, IDNF)):
                            sems.inc(g.dma_start(out=dst[:, :], in_=src[:, :]),
                                     "init", 16)
                        for r in range(4):
                            sems.inc(g.dma_start(out=kt_sb[:, r * 1024:(r + 1) * 1024],
                                                 in_=KVG[r, 0, :, :]), "init", 16)
                            sems.inc(g.dma_start(out=vt_sb[:, r * 1024:(r + 1) * 1024],
                                                 in_=KVG[r, 1, :, :]), "init", 16)
                        for pt in range(NPT):
                            sems.inc(g.dma_start(out=y_sb[:, pt * 1024:(pt + 1) * 1024],
                                                 in_=Y2D[pt, :, :]), "init", 16)

                    @blk.vector
                    def _(v):
                        v.wait_ge(sems["init"], sems.n["init"])
                        v.tensor_scalar(b2h_sb[:, :], b2h_sb[:, :], 0.5, None,
                                        ALU.mult)

                order = [(r, i) for r in range(4) for i in range(8)]
                with nc.Block() as blk:
                    @blk.tensor
                    def _(t):
                        for n, (r, i) in enumerate(order):
                            if n >= 2:
                                t.wait_ge(sems["vkev"], n - 1)
                            sems.inc(t.transpose(
                                tp[n % 2][:, :],
                                vt_sb[:, r * 1024 + i * 128:r * 1024 + (i + 1) * 128],
                                idnb_sb[:, :]), "vktr")
                        for ch in range(4):
                            t.wait_ge(sems["knsq"], ch + 1)
                            for h2 in range(2):
                                n = 2 * ch + h2
                                if h2 == 1:
                                    t.wait_ge(sems["knev"], 3 * ch + 1)
                                elif ch >= 1:
                                    t.wait_ge(sems["knev"], 3 * ch - 1)
                                sems.inc(t.matmul(knp[:, :], onesbf_sb[:, :],
                                                  qsq_sb[:, h2 * 512:(h2 + 1) * 512],
                                                  start=True, stop=True), "knmm")

                    @blk.vector
                    def _(v):
                        for n, (r, i) in enumerate(order):
                            kt = 4 * i + r
                            v.wait_ge(sems["vktr"], n + 1)
                            sems.inc(v.tensor_copy(vk_sb[:, kt * 128:(kt + 1) * 128],
                                                   tp[n % 2][:, :]), "vkev")
                        for ch in range(4):
                            for h2 in range(2):
                                n = 2 * ch + h2
                                v.wait_ge(sems["knmm"], n + 1)
                                if h2 == 0 and ch >= 1:
                                    v.wait_ge(sems["knev"], 3 * ch)
                                sems.inc(v.tensor_copy(
                                    nrow_sb[:, h2 * 512:(h2 + 1) * 512], knp[:, :]),
                                    "knev")
                            v.wait_ge(sems["knev"], 3 * ch + 2)
                            sems.inc(v.reduce_max(bsc_sb[:, 64 + ch:65 + ch],
                                                  nrow_sb[:, 0:1024], axis=AX), "knev")
                        v.wait_ge(sems["knev"], 12)
                        sems.inc(v.reduce_max(bsc_sb[:, 0:1], bsc_sb[:, 64:68],
                                              axis=AX), "knev")

                    @blk.scalar
                    def _(s):
                        for ch in range(4):
                            if ch >= 1:
                                s.wait_ge(sems["knmm"], 2 * ch)
                            sems.inc(s.activation(qsq_sb[:, :],
                                                  kt_sb[:, ch * 1024:(ch + 1) * 1024],
                                                  AF.Square), "knsq")

            with ExitStack() as esQ:
                qp = [esQ.enter_context(nc.psum_tensor(f"qp{i}", [128, 512], F32))
                      for i in range(8)]
                with nc.Block() as blk:
                    @blk.sync
                    def _(s):
                        for g in range(16):
                            for k in range(32):
                                n = g * 32 + k
                                if n >= 2:
                                    s.wait_ge(sems["qwc"], n - 1)
                                sems.inc(s.dma_start(
                                    out=wqs[n % 2][:, :],
                                    in_=WQP[g, :, k * 256:(k + 1) * 256]),
                                    f"qwld{n % 2}", 16)

                    @blk.scalar
                    def _(s):
                        for g in range(16):
                            for k in range(32):
                                n = g * 32 + k
                                s.wait_ge(sems[f"qwld{n % 2}"], 16 * (n // 2 + 1))
                                if n >= 2:
                                    s.wait_ge(sems["qwu"], n - 1)
                                sems.inc(s.activation(wqb[n % 2][:, :], wqs[n % 2][:, :],
                                                      AF.Copy), "qwc")
                            s.wait_ge(sems["qwu"], 32 * (g + 1))
                            for m2 in range(2):
                                for h2 in range(2):
                                    hh = 2 * g + m2
                                    c0 = hh * 1024 + h2 * 512
                                    last = s.activation(
                                        q_sb[:, c0:c0 + 512],
                                        qp[4 * (g % 2) + 2 * m2 + h2][:, :], AF.Copy,
                                        scale=SCALE)
                            sems.inc(last, "qev")

                    @blk.tensor
                    def _(t):
                        for g in range(16):
                            if g >= 2:
                                t.wait_ge(sems["qev"], g - 1)
                            for k in range(32):
                                n = g * 32 + k
                                t.wait_ge(sems["qwc"], n + 1)
                                for m2 in range(2):
                                    for h2 in range(2):
                                        last = t.matmul(
                                            qp[4 * (g % 2) + 2 * m2 + h2][:, :],
                                            wqb[n % 2][:, m2 * 128:(m2 + 1) * 128],
                                            y_sb[:, k * 1024 + h2 * 512:k * 1024 + (h2 + 1) * 512],
                                            start=(k == 0), stop=(k == 31))
                                sems.inc(last, "qwu")

            with ExitStack() as esN:
                np1 = esN.enter_context(nc.psum_tensor("np1", [1, 512], F32))
                bbp = esN.enter_context(nc.psum_tensor("bbp", [128, 32], F32))
                with nc.Block() as blk:
                    @blk.scalar
                    def _(s):
                        for h in range(NH):
                            if h >= 1:
                                s.wait_ge(sems["qnmm"], 2 * h)
                            sems.inc(s.activation(qsq_sb[:, :],
                                                  q_sb[:, h * 1024:(h + 1) * 1024],
                                                  AF.Square), "qnsq")
                        s.wait_ge(sems["bch"], 1)
                        sems.inc(s.activation(bsc_sb[:, 32:64], bsc_sb[:, 32:64],
                                              AF.Sqrt), "bch")

                    @blk.tensor
                    def _(t):
                        for h in range(NH):
                            t.wait_ge(sems["qnsq"], h + 1)
                            for h2 in range(2):
                                if h2 == 1:
                                    t.wait_ge(sems["qnev"], 3 * h + 1)
                                elif h >= 1:
                                    t.wait_ge(sems["qnev"], 3 * h - 1)
                                sems.inc(t.matmul(np1[:, :], onesbf_sb[:, :],
                                                  qsq_sb[:, h2 * 512:(h2 + 1) * 512],
                                                  start=True, stop=True), "qnmm")
                        t.wait_ge(sems["bch"], 2)
                        sems.inc(t.matmul(bbp[:, :], onesf_sb[:, :], bsc_sb[:, 32:64],
                                          start=True, stop=True), "qnmm")

                    @blk.vector
                    def _(v):
                        for h in range(NH):
                            for h2 in range(2):
                                n = 2 * h + h2
                                v.wait_ge(sems["qnmm"], n + 1)
                                if h2 == 0 and h >= 1:
                                    v.wait_ge(sems["qnev"], 3 * h)
                                sems.inc(v.tensor_copy(
                                    nrow_sb[:, h2 * 512:(h2 + 1) * 512], np1[:, :]),
                                    "qnev")
                            v.wait_ge(sems["qnev"], 3 * h + 2)
                            sems.inc(v.reduce_max(bsc_sb[:, 1 + h:2 + h],
                                                  nrow_sb[:, 0:1024], axis=AX), "qnev")
                        v.wait_ge(sems["qnev"], 3 * NH)
                        sems.inc(v.tensor_scalar(bsc_sb[:, 32:64], bsc_sb[:, 1:33],
                                                 bsc_sb[:, 0:1], None, ALU.mult),
                                 "bch")

                    @blk.scalar
                    def _(s):
                        s.wait_ge(sems["qnmm"], 2 * NH + 1)
                        sems.inc(s.activation(bcol_sb[:, :], bbp[:, :], AF.Copy),
                                 "bch")

                    @blk.vector
                    def _(v):
                        v.wait_ge(sems["bch"], 3)
                        for h in range(NH):
                            sems.inc(v.tensor_scalar(ba_sb[:, h * N_SC:(h + 1) * N_SC],
                                                     ot_sb[:, h * N_SC:(h + 1) * N_SC],
                                                     bcol_sb[:, h:h + 1], None,
                                                     ALU.subtract), "bch")

            chunks = []
            slot_of = {}
            g = 0
            for h in range(NH):
                for i in range(8):
                    for c in range(i + 1):
                        chunks.append((h, i, c, g))
                        if c == i:
                            slot_of[g] = (h, i, h * 8 + i)
                        g += 1
            NG = g

            with ExitStack() as esE:
                scp = [esE.enter_context(nc.psum_tensor(f"scp{i}", [128, 512], F32))
                       for i in range(2)]
                ptp = [esE.enter_context(nc.psum_tensor(f"ptp{i}", [128, 512], BF16))
                       for i in range(2)]
                pvp = [esE.enter_context(nc.psum_tensor(f"pvp{i}", [128, 128], F32))
                       for i in range(2)]
                nrp = esE.enter_context(nc.psum_tensor("nrp", [1, 128], F32))
                rbp = esE.enter_context(nc.psum_tensor("rbp", [128, 128], F32))
                with nc.Block() as blk:
                    b_scmm = sems.n["scmm"]; b_scv = sems.n["scv"]
                    b_sexp = sems.n["sexp"]; b_strc = sems.n["strc"]
                    b_ptev = sems.n["ptev"]; b_spv = sems.n["spv"]
                    b_nsum = sems.n["nsum"]; b_nrow1 = sems.n["nrow1"]
                    b_nrec = sems.n["nrec"]; b_nrb2 = sems.n["nrb2"]
                    b_nrbc = sems.n["nrbc"]; b_ndone = sems.n["ndone"]

                    def kt_rhs(c):
                        return bass.AP(kt_sb, c * 128,
                                       [[4096, 128], [1024, 4], [1, 128]])

                    @blk.tensor
                    def _(t):
                        for gg in range(NG + 6):
                            if gg < NG:
                                h, i, c, _ = chunks[gg]
                                if gg >= 2:
                                    t.wait_ge(sems["scv"], b_scv + gg - 1)
                                sems.inc(t.matmul(
                                    scp[gg % 2][:, :],
                                    q_sb[:, h * 1024 + i * 128:h * 1024 + (i + 1) * 128],
                                    kt_rhs(c), start=True, stop=True), "scmm")
                            gp = gg - 2
                            if 0 <= gp < NG:
                                t.wait_ge(sems["sexp"], b_sexp + gp + 1)
                                if gp >= 2:
                                    t.wait_ge(sems["ptev"], b_ptev + gp - 1)
                                for bb in range(4):
                                    last = t.transpose(
                                        ptp[gp % 2][:, bb * 128:(bb + 1) * 128],
                                        pr_sb[gp % 4][:, bb * 128:(bb + 1) * 128],
                                        idnb_sb[:, :])
                                sems.inc(last, "strc")
                                if gp in slot_of:
                                    h2_, i2_, sl = slot_of[gp]
                                    t.wait_ge(sems["ptev"], b_ptev + gp + 1)
                                    if sl >= 2:
                                        t.wait_ge(sems["ndone"], b_ndone + sl - 1)
                                    nkt = 4 * (i2_ + 1)
                                    for kt in range(nkt):
                                        last = t.matmul(
                                            pvp[sl % 2][:, :],
                                            vk_sb[:, kt * 128:(kt + 1) * 128],
                                            pt_sb[:, kt * 128:(kt + 1) * 128],
                                            start=(kt == 0), stop=(kt == nkt - 1))
                                    sems.inc(last, "spv")
                                    t.wait_ge(sems["nsum"], b_nsum + sl + 1)
                                    if sl >= 1:
                                        t.wait_ge(sems["nrec"], b_nrec + sl)
                                    sems.inc(t.matmul(nrp[:, :], ssum_sb[:, :],
                                                      idnf_sb[:, :], start=True,
                                                      stop=True), "nrow1")
                            gm = gg - 3
                            if 0 <= gm < NG and gm in slot_of:
                                _, _, sl = slot_of[gm]
                                t.wait_ge(sems["nrec"], b_nrec + sl + 1)
                                if sl >= 1:
                                    t.wait_ge(sems["nrbc"], b_nrbc + sl)
                                sems.inc(t.matmul(rbp[:, :], onesf_sb[:, :],
                                                  srow_sb[:, :], start=True,
                                                  stop=True), "nrb2")

                    @blk.vector
                    def _(v):
                        for gg in range(NG + 6):
                            if gg < NG:
                                h, i, c, _ = chunks[gg]
                                if i == 0 and c == 0:
                                    if h >= 1:
                                        v.wait_ge(sems["scv"], b_scv + N_SC * h)
                                    v.scalar_tensor_tensor(rm_sb[:, :], r0_sb[:, :],
                                                           float(SLOPES[h]),
                                                           mk_sb[:, :], ALU.mult,
                                                           ALU.add)
                                    sems.inc(v.tensor_scalar(relh_sb[:, :], r0_sb[:, :],
                                                             float(SLOPES[h]), None,
                                                             ALU.mult), "relw")
                                    v.wait_ge(sems["relw"], h + 1)
                                v.wait_ge(sems["scmm"], b_scmm + gg + 1)
                                if gg >= 2:
                                    v.wait_ge(sems["sexp"], b_sexp + gg - 1)
                                rel = rm_sb if c == i else relh_sb
                                sems.inc(v.scalar_tensor_tensor(
                                    sc2_sb[gg % 2][:, :], scp[gg % 2][:, :], 1.0,
                                    rel[:, :], ALU.mult, ALU.add), "scv")
                            gp = gg - 2
                            if 0 <= gp < NG:
                                h2_, i2_, c2_, _ = chunks[gp]
                                v.wait_ge(sems["strc"], b_strc + gp + 1)
                                if c2_ == 0 and (h2_ * 8 + i2_) >= 1:
                                    v.wait_ge(sems["spv"], b_spv + h2_ * 8 + i2_)
                                sems.inc(v.tensor_copy(
                                    pt_sb[:, c2_ * 512:(c2_ + 1) * 512],
                                    ptp[gp % 2][:, :]), "ptev")
                                if gp in slot_of:
                                    _, _, sl = slot_of[gp]
                                    v.wait_ge(sems["sexp"], b_sexp + gp + 1)
                                    if sl >= 1:
                                        v.wait_ge(sems["nrow1"], b_nrow1 + sl)
                                    i0 = (h2_ % 2) * N_SC + _scidx(i2_, 0)
                                    sems.inc(v.reduce_sum(ssum_sb[:, :],
                                                          sums_sb[:, i0:i0 + i2_ + 1],
                                                          axis=AX), "nsum")
                            gr = gg - 3
                            if 0 <= gr < NG and gr in slot_of:
                                _, _, sl = slot_of[gr]
                                v.wait_ge(sems["nrow1"], b_nrow1 + sl + 1)
                                sems.inc(v.reciprocal(srow_sb[:, :], nrp[:, :]),
                                         "nrec")
                            gf = gg - 5
                            if 0 <= gf < NG and gf in slot_of:
                                h2_, i2_, sl = slot_of[gf]
                                v.wait_ge(sems["nrbc"], b_nrbc + sl + 1)
                                v.wait_ge(sems["spv"], b_spv + sl + 1)
                                c0 = h2_ * 1024 + i2_ * 128
                                sems.inc(v.tensor_mul(q_sb[:, c0:c0 + 128],
                                                      pvp[sl % 2][:, :], rb_sb[:, :]),
                                         "ndone")

                    @blk.scalar
                    def _(s):
                        for gg in range(NG + 6):
                            if gg < NG:
                                h, i, c, _ = chunks[gg]
                                if i == 0 and c == 0 and h >= 2:
                                    s.wait_ge(sems["nsum"], b_nsum + 8 * (h - 1))
                                s.wait_ge(sems["scv"], b_scv + gg + 1)
                                if gg >= 4:
                                    s.wait_ge(sems["strc"], b_strc + gg - 3)
                                col = h * N_SC + _scidx(i, c)
                                scol = (h % 2) * N_SC + _scidx(i, c)
                                sems.inc(s.activation(
                                    pr_sb[gg % 4][:, :], sc2_sb[gg % 2][:, :], AF.Exp,
                                    bias=ba_sb[:, col:col + 1],
                                    accum_out=sums_sb[:, scol:scol + 1]), "sexp")
                            gb = gg - 4
                            if 0 <= gb < NG and gb in slot_of:
                                _, _, sl = slot_of[gb]
                                s.wait_ge(sems["nrb2"], b_nrb2 + sl + 1)
                                if sl >= 1:
                                    s.wait_ge(sems["ndone"], b_ndone + sl)
                                sems.inc(s.activation(rb_sb[:, :], rbp[:, :], AF.Copy),
                                         "nrbc")

            with ExitStack() as esW:
                xsw = [esW.enter_context(nc.sbuf_tensor(f"xsw{i}", [128, 1024], F32))
                       for i in range(2)]
                op = [esW.enter_context(nc.psum_tensor(f"wop{i}", [128, 512], F32))
                      for i in range(8)]
                with nc.Block() as blk:
                    @blk.sync
                    def _(s):
                        for g in range(16):
                            for k in range(32):
                                n = g * 32 + k
                                if n >= 2:
                                    s.wait_ge(sems["wwc"], n - 1)
                                sems.inc(s.dma_start(
                                    out=wqs[n % 2][:, :],
                                    in_=WOP[g, :, k * 256:(k + 1) * 256]),
                                    f"wwld{n % 2}", 16)

                    @blk.scalar
                    def _(s):
                        for g in range(16):
                            for k in range(32):
                                n = g * 32 + k
                                s.wait_ge(sems[f"wwld{n % 2}"], 16 * (n // 2 + 1))
                                if n >= 2:
                                    s.wait_ge(sems["wwu"], n - 1)
                                sems.inc(s.activation(wqb[n % 2][:, :],
                                                      wqs[n % 2][:, :], AF.Copy),
                                         "wwc")

                    @blk.tensor
                    def _(t):
                        for g in range(16):
                            if g >= 1:
                                t.wait_ge(sems["wxadd"], 2 * (g - 1))
                            for k in range(32):
                                n = g * 32 + k
                                t.wait_ge(sems["wwc"], n + 1)
                                for m2 in range(2):
                                    for h2 in range(2):
                                        last = t.matmul(
                                            op[4 * (g % 2) + 2 * m2 + h2][:, :],
                                            wqb[n % 2][:, m2 * 128:(m2 + 1) * 128],
                                            q_sb[:, k * 1024 + h2 * 512:k * 1024 + (h2 + 1) * 512],
                                            start=(k == 0), stop=(k == 31))
                                sems.inc(last, "wwu")

                    @blk.gpsimd
                    def _(g):
                        for om in range(32):
                            if om >= 2:
                                g.wait_ge(sems[f"wst2{om % 2}"], 16 * (om // 2))
                            sems.inc(g.dma_start(out=xsw[om % 2][:, :],
                                                 in_=X1[om, :, :]),
                                     f"wxld{om % 2}", 16)
                            g.wait_ge(sems["wxadd"], om + 1)
                            sems.inc(g.dma_start(out=XB[om, :, :],
                                                 in_=xsw[om % 2][:, :]),
                                     f"wst2{om % 2}", 16)

                    @blk.vector
                    def _(v):
                        for om in range(32):
                            g2, m2 = om // 2, om % 2
                            v.wait_ge(sems["wwu"], 32 * (g2 + 1))
                            v.wait_ge(sems[f"wxld{om % 2}"], 16 * (om // 2 + 1))
                            for h2 in range(2):
                                last = v.tensor_add(
                                    xsw[om % 2][:, h2 * 512:(h2 + 1) * 512],
                                    xsw[om % 2][:, h2 * 512:(h2 + 1) * 512],
                                    op[4 * (g2 % 2) + 2 * m2 + h2][:, :])
                            sems.inc(last, "wxadd")

        with ExitStack() as esF:
            sbF = lambda n, sh, dt: esF.enter_context(nc.sbuf_tensor(n, sh, dt))
            y2_sb = sbF("y3", [128, NPT * 1024], BF16)
            p_sb = sbF("p", [128, 8 * 1024], BF16)
            zacc = sbF("zacc", [128, 8 * 1024], F32)
            bufs = _common_bufs(nc, esF, pfx="f_")
            bufs["b1p"] = b1p_sb
            bufs["b2h"] = b2h_sb
            bufs["ones_bf"] = onesbf_sb
            bufs["ones_f32r"] = onesf_sb
            _emit_ln(nc, sems, bufs, XB, y2_sb,
                     pre=[("wst20", sems.n["wst20"]), ("wst21", sems.n["wst21"])])
            _emit_ffn(nc, sems, bufs, y2_sb, p_sb, zacc, XB, OUT, wd, is_final=True)
    return nc



_CACHE = {}


def kernel(L0_attn_hidden_states, L0_attn_Wq, L0_attn_Wkv, L0_attn_Wo,
           L0_ffn_U1, L0_ffn_V1, L0_ffn_U2, L0_ffn_V2, L0_ffn_b1, L0_ffn_b2):
    x = np.asarray(L0_attn_hidden_states, dtype=np.float32)
    wp = _pack_weights(np.asarray(L0_attn_Wq, np.float32),
                       np.asarray(L0_attn_Wkv, np.float32),
                       np.asarray(L0_attn_Wo, np.float32),
                       np.asarray(L0_ffn_U1, np.float32),
                       np.asarray(L0_ffn_V1, np.float32),
                       np.asarray(L0_ffn_U2, np.float32),
                       np.asarray(L0_ffn_V2, np.float32),
                       np.asarray(L0_ffn_b1, np.float32),
                       np.asarray(L0_ffn_b2, np.float32))
    consts = [_consts_for_core(j) for j in range(4)]
    rows = [_core_token_rows(j) for j in range(4)]

    if "A" not in _CACHE:
        _CACHE["A"] = build_A()
    if "B" not in _CACHE:
        _CACHE["B"] = build_B()

    in_A = []
    for c in range(8):
        b, j = c // 4, c % 4
        xc = np.ascontiguousarray(x[b][rows[j], :].T.reshape(NPT, 128, TOK))
        in_A.append({"x": xc, "u1p": wp["u1p"], "v1u2p": wp["v1u2p"],
                     "v2p": wp["v2p"], "wkvp": wp["wkvp"], "b1p": wp["b1p"],
                     "b2p": wp["b2p"], "ones_bf": consts[j]["ones_bf"],
                     "ones_f32r": consts[j]["ones_f32r"]})
    resA = run_bass_kernel_spmd(_CACHE["A"], in_A, core_ids=list(range(8)))

    in_B = []
    for c in range(8):
        b, j = c // 4, c % 4
        kvg = np.stack([np.asarray(resA.results[b * 4 + r]["kv"]) for r in range(4)])
        cc = consts[j]
        in_B.append({"x1": np.asarray(resA.results[c]["x1"]),
                     "y2": np.asarray(resA.results[c]["y2"]),
                     "kvg": np.ascontiguousarray(kvg),
                     "wqp": wp["wqp"], "wop": wp["wop"], "u1p": wp["u1p"],
                     "v1u2p": wp["v1u2p"], "v2p": wp["v2p"], "b1p": wp["b1p"],
                     "b2p": wp["b2p"], "r0": cc["r0"], "mask1": cc["mask1"],
                     "otab": cc["otab"], "idn_bf": cc["idn_bf"],
                     "idn_f32": cc["idn_f32"], "ones_bf": cc["ones_bf"],
                     "ones_f32r": cc["ones_f32r"]})
    resB = run_bass_kernel_spmd(_CACHE["B"], in_B, core_ids=list(range(8)))

    out = np.empty((NB, S, D), dtype=np.float32)
    for c in range(8):
        b, j = c // 4, c % 4
        oc = np.asarray(resB.results[c]["out"]).reshape(D, TOK)
        out[b][rows[j], :] = oc.T
    return out


# version 02
# speedup vs baseline: 1.0338x; 1.0338x over previous
import math
from contextlib import ExitStack

import numpy as np
import ml_dtypes

import concourse.bass as bass
import concourse.mybir as mybir
from concourse.bass_utils import run_bass_kernel_spmd

F32 = mybir.dt.float32
BF16 = mybir.dt.bfloat16
AF = mybir.ActivationFunctionType
ALU = mybir.AluOpType
AX = mybir.AxisListType.X
bfdt = ml_dtypes.bfloat16

NB, S, D = 2, 4096, 4096
NH, HD = 32, 128
INTER, RANK = 11008, 1024
TOK = 1024
NPT = 32
NI = 86
SCALE = 1.0 / math.sqrt(HD)
EPS = 1e-5
NEG = -1e30
SLOPES = [2.0 ** (-(i + 1) / 4.0) for i in range(NH)]


def _scidx(i, c):
    return i * (i + 1) // 2 + c


N_SC = 36



def _pack_weights(Wq, Wkv, Wo, U1, V1, U2, V2, b1, b2):
    p = {}
    p["u1p"] = np.ascontiguousarray(
        U1.reshape(32, 128, 8, 128).transpose(2, 1, 0, 3).reshape(8, 128, 4096))
    v1t = V1.reshape(8, 128, 172, 128).transpose(2, 1, 0, 3).reshape(172, 128, 1024)
    u2t = U2.reshape(86, 128, 8, 128).reshape(86, 128, 1024)
    p["v1u2p"] = np.ascontiguousarray(
        np.concatenate([v1t[:86], v1t[86:], u2t], axis=2))
    p["v2p"] = np.ascontiguousarray(
        V2.reshape(8, 128, 32, 128).transpose(2, 1, 0, 3).reshape(32, 128, 1024))
    p["wqp"] = np.ascontiguousarray(
        Wq.reshape(32, 128, 16, 256).transpose(2, 1, 0, 3).reshape(16, 128, 8192))
    p["wop"] = np.ascontiguousarray(
        Wo.reshape(32, 128, 16, 256).transpose(2, 1, 0, 3).reshape(16, 128, 8192))
    p["wkvp"] = np.ascontiguousarray(
        Wkv.reshape(32, 128, 256).transpose(1, 0, 2).reshape(128, 8192))
    p["b1p"] = np.ascontiguousarray(b1.reshape(172, 128).T)
    p["b2p"] = np.ascontiguousarray(b2.reshape(32, 128).T)
    return p


def _consts_for_core(j):
    c = {}
    dq = np.arange(128, dtype=np.float32)[:, None]
    dk = np.arange(512, dtype=np.float32)[None, :]
    c["r0"] = np.ascontiguousarray((dk - dq).astype(np.float32))
    c["mask1"] = np.where(dk - dq <= 128.0 * j, 0.0, NEG).astype(np.float32)
    ot = np.zeros((NH, N_SC), dtype=np.float32)
    for h in range(NH):
        for i in range(8):
            for cc in range(i + 1):
                ot[h, _scidx(i, cc)] = SLOPES[h] * (512.0 * cc - 512.0 * i - 128.0 * j)
    c["otab"] = np.ascontiguousarray(
        np.broadcast_to(ot.reshape(1, NH * N_SC), (128, NH * N_SC)).astype(np.float32))
    c["idn_bf"] = np.eye(128, dtype=bfdt)
    c["idn_f32"] = np.eye(128, dtype=np.float32)
    c["ones_bf"] = np.ones((128, 1), dtype=bfdt)
    c["ones_f32r"] = np.ones((1, 128), dtype=np.float32)
    return c


def _core_token_rows(j):
    idx = []
    for i in range(8):
        t0 = 128 * (4 * i + j)
        idx.extend(range(t0, t0 + 128))
    return np.array(idx, dtype=np.int64)




class Sems:
    def __init__(self, nc, es, names):
        self.h = {n: es.enter_context(nc.semaphore(n)) for n in names}
        self.n = {n: 0 for n in names}

    def inc(self, instr, name, k=1):
        instr.then_inc(self.h[name], k)
        self.n[name] += k
        return self.n[name]

    def __getitem__(self, name):
        return self.h[name]


def _common_bufs(nc, es, pfx=""):
    sb = lambda n, sh, dt: es.enter_context(nc.sbuf_tensor(pfx + n, sh, dt))
    bufs = {
        "xst": [sb(f"xst{i}", [128, 1024], F32) for i in range(2)],
        "xbf": [sb(f"xbf{i}", [128, 1024], BF16) for i in range(2)],
        "xsq": [sb(f"xsq{i}", [128, 1024], BF16) for i in range(2)],
        "row": sb("row", [1, 4096], F32),
        "mb_sb": sb("mb", [128, 1024], F32),
        "ib_sb": sb("ib", [128, 1024], F32),
        "u1st": [sb(f"u1st{i}", [128, 512], F32) for i in range(2)],
        "u1bf": [sb(f"u1bf{i}", [128, 512], BF16) for i in range(2)],
        "wst": [sb(f"wst{i}", [128, 1536], F32) for i in range(2)],
        "wbf": [sb(f"wbf{i}", [128, 3072], BF16) for i in range(2)],
        "v2st": [sb(f"v2st{i}", [128, 512], F32) for i in range(2)],
        "v2bf": [sb(f"v2bf{i}", [128, 1024], BF16) for i in range(2)],
        "sg_sb": sb("sg", [128, 1024], F32),
        "ub_sb": sb("ub", [128, 1024], F32),
        "act_bf": [sb(f"act{i}", [128, 1024], BF16) for i in range(2)],
        "b1p": sb("b1ps", [128, 172], F32),
        "b2h": sb("b2h", [128, 32], F32),
        "ones_bf": sb("onesbf", [128, 1], BF16),
        "ones_f32r": sb("onesf", [1, 128], F32),
    }
    bufs["xst2"] = bufs["xst"]
    bufs["t_sb"] = [bufs["sg_sb"], bufs["ub_sb"]]
    return bufs


LN_SEMS = ["lnld0", "lnld1", "lncast", "lnmm", "lnfin", "lnap",
           "lnld20", "lnld21"]
FFN_SEMS = ["fu1ld0", "fu1ld1", "fu1c", "fu1u", "fpev",
            "fwld0", "fwld1", "fwc",
            "fhmm", "fhdr", "fact", "fzmm", "fzd",
            "fv2ld0", "fv2ld1", "fv2c",
            "fomm", "ft", "fxld0", "fxld1", "fxadd", "fst0", "fst1", "fzcast"]


_LN_N = [0]


def _emit_ln(nc, sm, bufs, src_dram, y_sb, pre=None):
    _LN_N[0] += 1
    tag = f"_{_LN_N[0]}"
    with ExitStack() as pse:
        psf = lambda n, sh: pse.enter_context(nc.psum_tensor(n + tag, sh, F32))
        sx = [psf(f"lnsx{i}", [1, 512]) for i in range(2)]
        sq = [psf(f"lnsq{i}", [1, 512]) for i in range(2)]
        bc = [psf(f"lnbc{i}", [128, 512]) for i in range(4)]
        xst, xbf, xsq = bufs["xst"], bufs["xbf"], bufs["xsq"]
        row, mb_sb, ib_sb = bufs["row"], bufs["mb_sb"], bufs["ib_sb"]
        ones_bf, ones_f32r = bufs["ones_bf"], bufs["ones_f32r"]
        b_ld = [sm.n["lnld0"], sm.n["lnld1"]]
        b_ld2 = [sm.n["lnld20"], sm.n["lnld21"]]
        b_cast = sm.n["lncast"]; b_mm = sm.n["lnmm"]
        b_fin = sm.n["lnfin"]; b_ap = sm.n["lnap"]

        with nc.Block() as blk:
            @blk.gpsimd
            def _(g):
                for pt in range(NPT):
                    if pt == 0 and pre is not None:
                        for pn, pv in pre:
                            g.wait_ge(sm[pn], pv)
                    if pt >= 2:
                        g.wait_ge(sm["lncast"], b_cast + pt - 1)
                    sm.inc(g.dma_start(out=xst[pt % 2][:, :], in_=src_dram[pt, :, :]),
                           f"lnld{pt % 2}", 16)

            @blk.sync
            def _(g):
                for pt in range(NPT):
                    if pt < 2:
                        g.wait_ge(sm["lncast"], b_cast + 31 + pt)
                    else:
                        g.wait_ge(sm["lnap"], b_ap + 2 * (pt - 1))
                    sm.inc(g.dma_start(out=xst[pt % 2][:, :], in_=src_dram[pt, :, :]),
                           f"lnld2{pt % 2}", 16)

            @blk.scalar
            def _(s):
                for pt in range(NPT):
                    s.wait_ge(sm[f"lnld{pt % 2}"], b_ld[pt % 2] + 16 * (pt // 2 + 1))
                    if pt >= 2:
                        s.wait_ge(sm["lnmm"], b_mm + pt - 1)
                    s.activation(xbf[pt % 2][:, :], xst[pt % 2][:, :], AF.Copy)
                    sm.inc(s.activation(xsq[pt % 2][:, :], xst[pt % 2][:, :],
                                        AF.Square), "lncast")
                s.wait_ge(sm["lnmm"], b_mm + NPT)
                for h2 in range(2):
                    last = s.activation(row[:, h2 * 512:(h2 + 1) * 512], sx[h2][:, :],
                                        AF.Copy, scale=1.0 / 4096.0)
                sm.inc(last, "lnfin")
                s.wait_ge(sm["lnfin"], b_fin + 4)
                sm.inc(s.activation(row[:, 3072:4096], row[:, 2048:3072], AF.Sqrt),
                       "lnfin")

            @blk.tensor
            def _(t):
                for pt in range(NPT):
                    t.wait_ge(sm["lncast"], b_cast + pt + 1)
                    for h2 in range(2):
                        c0 = h2 * 512
                        t.matmul(sx[h2][:, :], ones_bf[:, :], xbf[pt % 2][:, c0:c0 + 512],
                                 start=(pt == 0), stop=(pt == NPT - 1))
                        last = t.matmul(sq[h2][:, :], ones_bf[:, :],
                                        xsq[pt % 2][:, c0:c0 + 512],
                                        start=(pt == 0), stop=(pt == NPT - 1))
                    sm.inc(last, "lnmm")
                t.wait_ge(sm["lnfin"], b_fin + 6)
                for h2 in range(2):
                    t.matmul(bc[h2][:, :], ones_f32r[:, :],
                             row[:, h2 * 512:(h2 + 1) * 512], start=True, stop=True)
                    last = t.matmul(bc[2 + h2][:, :], ones_f32r[:, :],
                                    row[:, 1024 + h2 * 512:1024 + (h2 + 1) * 512],
                                    start=True, stop=True)
                sm.inc(last, "lnmm")

            @blk.vector
            def _(v):
                v.wait_ge(sm["lnfin"], b_fin + 1)
                sm.inc(v.tensor_mul(row[:, 1024:2048], row[:, 0:1024], row[:, 0:1024]),
                       "lnfin")
                v.wait_ge(sm["lnfin"], b_fin + 2)
                for h2 in range(2):
                    last = v.scalar_tensor_tensor(
                        row[:, 2048 + h2 * 512:2048 + (h2 + 1) * 512], sq[h2][:, :],
                        1.0 / 4096.0, row[:, 1024 + h2 * 512:1024 + (h2 + 1) * 512],
                        ALU.mult, ALU.subtract)
                sm.inc(last, "lnfin")
                v.wait_ge(sm["lnfin"], b_fin + 3)
                sm.inc(v.tensor_scalar(row[:, 2048:3072], row[:, 2048:3072], EPS,
                                       None, ALU.add), "lnfin")
                v.wait_ge(sm["lnfin"], b_fin + 5)
                sm.inc(v.reciprocal(row[:, 1024:2048], row[:, 3072:4096]), "lnfin")

            @blk.scalar
            def _(s):
                s.wait_ge(sm["lnmm"], b_mm + NPT + 1)
                for h2 in range(2):
                    s.activation(mb_sb[:, h2 * 512:(h2 + 1) * 512], bc[h2][:, :], AF.Copy)
                    last = s.activation(ib_sb[:, h2 * 512:(h2 + 1) * 512],
                                        bc[2 + h2][:, :], AF.Copy)
                sm.inc(last, "lnfin")

            @blk.vector
            def _(v):
                v.wait_ge(sm["lnfin"], b_fin + 7)
                for pt in range(NPT):
                    v.wait_ge(sm[f"lnld2{pt % 2}"], b_ld2[pt % 2] + 16 * (pt // 2 + 1))
                    sm.inc(v.tensor_sub(xst[pt % 2][:, :], xst[pt % 2][:, :],
                                        mb_sb[:, :]), "lnap")
                    v.wait_ge(sm["lnap"], b_ap + 2 * pt + 1)
                    sm.inc(v.tensor_mul(y_sb[:, pt * 1024:(pt + 1) * 1024],
                                        xst[pt % 2][:, :], ib_sb[:, :]), "lnap")


def _emit_ffn(nc, sm, bufs, y_sb, p_sb, zacc, xsrc, xdst, wd, is_final):
    with ExitStack() as pse:
        psf = lambda n, sh: pse.enter_context(nc.psum_tensor(n, sh, F32))
        q4 = [psf(f"fq{i}", [128, 512]) for i in range(4)]
        zp = [psf(f"fzp{i}", [128, 1024]) for i in range(2)]
        u1st, u1bf = bufs["u1st"], bufs["u1bf"]
        wst, wbf = bufs["wst"], bufs["wbf"]
        v2st, v2bf = bufs["v2st"], bufs["v2bf"]
        sg_sb, ub_sb = bufs["sg_sb"], bufs["ub_sb"]
        act_bf, t_sb, xst2 = bufs["act_bf"], bufs["t_sb"], bufs["xst2"]
        b1p, b2h = bufs["b1p"], bufs["b2h"]
        b = {k: sm.n[k] for k in FFN_SEMS}

        with nc.Block() as blk:
            @blk.sync
            def _(s):
                for pm in range(8):
                    for kc in range(8):
                        n = pm * 8 + kc
                        if n >= 2:
                            s.wait_ge(sm["fu1c"], b["fu1c"] + n - 1)
                        sm.inc(s.dma_start(out=u1st[n % 2][:, :],
                                           in_=wd["u1p"][pm, :, kc * 512:(kc + 1) * 512]),
                               f"fu1ld{n % 2}", 16)
                for i in range(NI):
                    for hf in range(2):
                        n = 2 * i + hf
                        if n >= 2:
                            s.wait_ge(sm["fwc"], b["fwc"] + n - 1)
                        sm.inc(s.dma_start(
                            out=wst[n % 2][:, :],
                            in_=wd["v1u2p"][i, :, hf * 1536:(hf + 1) * 1536]),
                               f"fwld{n % 2}", 16)
                for om in range(34):
                    if om < 32:
                        for hf in range(2):
                            n = 2 * om + hf
                            if n >= 2:
                                s.wait_ge(sm["fv2c"], b["fv2c"] + n - 1)
                            sm.inc(s.dma_start(
                                out=v2st[n % 2][:, :],
                                in_=wd["v2p"][om, :, hf * 512:(hf + 1) * 512]),
                                   f"fv2ld{n % 2}", 16)
                    ol = om - 1
                    if 0 <= ol < 32:
                        if ol >= 2:
                            s.wait_ge(sm[f"fst{ol % 2}"],
                                      b[f"fst{ol % 2}"] + 16 * (ol // 2))
                        sm.inc(s.dma_start(out=xst2[ol % 2][:, :],
                                           in_=xsrc[ol, :, :]),
                               f"fxld{ol % 2}", 16)
                    os_ = om - 2
                    if 0 <= os_ < 32:
                        s.wait_ge(sm["fxadd"], b["fxadd"] + os_ + 1)
                        sm.inc(s.dma_start(out=xdst[os_, :, :],
                                           in_=xst2[os_ % 2][:, :]),
                               f"fst{os_ % 2}", 16)
                if is_final:
                    s.wait_ge(sm["fst0"], b["fst0"] + 16 * 16)
                    s.wait_ge(sm["fst1"], b["fst1"] + 16 * 16)

            @blk.scalar
            def _(s):
                def u1cast(n):
                    s.wait_ge(sm[f"fu1ld{n % 2}"],
                              b[f"fu1ld{n % 2}"] + 16 * (n // 2 + 1))
                    if n >= 2:
                        s.wait_ge(sm["fu1u"], b["fu1u"] + n - 1)
                    sm.inc(s.activation(u1bf[n % 2][:, :], u1st[n % 2][:, :],
                                        AF.Copy), "fu1c")

                u1cast(0)
                u1cast(1)
                for pm in range(8):
                    for kc in range(8):
                        n = pm * 8 + kc
                        if n + 2 < 64:
                            u1cast(n + 2)
                    if pm >= 1:
                        s.wait_ge(sm["fu1u"], b["fu1u"] + 8 * pm)
                        for h2 in range(2):
                            c0 = (pm - 1) * 1024 + h2 * 512
                            last = s.activation(p_sb[:, c0:c0 + 512],
                                                q4[2 * ((pm - 1) % 2) + h2][:, :],
                                                AF.Copy)
                        sm.inc(last, "fpev")
                s.wait_ge(sm["fu1u"], b["fu1u"] + 64)
                for h2 in range(2):
                    c0 = 7 * 1024 + h2 * 512
                    last = s.activation(p_sb[:, c0:c0 + 512], q4[2 + h2][:, :], AF.Copy)
                sm.inc(last, "fpev")

                def wcast(n):
                    i2, hf = n // 2, n % 2
                    s.wait_ge(sm[f"fwld{n % 2}"],
                              b[f"fwld{n % 2}"] + 16 * (n // 2 + 1))
                    if i2 >= 2 and hf == 0:
                        s.wait_ge(sm["fzmm"], b["fzmm"] + 8 * (i2 - 1))
                    sm.inc(s.activation(wbf[i2 % 2][:, hf * 1536:(hf + 1) * 1536],
                                        wst[n % 2][:, :], AF.Copy), "fwc")

                for n in range(4):
                    wcast(n)
                for i in range(NI):
                    s.wait_ge(sm["fhmm"], b["fhmm"] + i + 1)
                    if i >= 1:
                        s.wait_ge(sm["fact"], b["fact"] + i)
                    for h2 in range(2):
                        c0 = h2 * 512
                        s.activation(sg_sb[:, c0:c0 + 512], q4[h2][:, :], AF.Silu,
                                     bias=b1p[:, i:i + 1])
                        last = s.activation(ub_sb[:, c0:c0 + 512], q4[2 + h2][:, :],
                                            AF.Identity, bias=b1p[:, 86 + i:87 + i])
                    sm.inc(last, "fhdr")
                    for hf in range(2):
                        n = 2 * (i + 2) + hf
                        if n < 2 * NI:
                            wcast(n)
                s.wait_ge(sm["fzd"], b["fzd"] + NI * 8)
                sm.inc(s.activation(p_sb[:, :], zacc[:, :], AF.Copy), "fzcast")

                def v2cast(n):
                    om2, hf = n // 2, n % 2
                    s.wait_ge(sm[f"fv2ld{n % 2}"],
                              b[f"fv2ld{n % 2}"] + 16 * (n // 2 + 1))
                    if om2 >= 2 and hf == 0:
                        s.wait_ge(sm["fomm"], b["fomm"] + om2 - 1)
                    sm.inc(s.activation(v2bf[om2 % 2][:, hf * 512:(hf + 1) * 512],
                                        v2st[n % 2][:, :], AF.Copy), "fv2c")

                for n in range(4):
                    v2cast(n)
                for om in range(32):
                    s.wait_ge(sm["fomm"], b["fomm"] + om + 1)
                    if om >= 2:
                        s.wait_ge(sm["fxadd"], b["fxadd"] + om - 1)
                    for h2 in range(2):
                        last = s.activation(t_sb[om % 2][:, h2 * 512:(h2 + 1) * 512],
                                            q4[2 * (om % 2) + h2][:, :], AF.Identity,
                                            scale=0.5, bias=b2h[:, om:om + 1])
                    sm.inc(last, "ft")
                    for hf in range(2):
                        n = 2 * (om + 2) + hf
                        if n < 64:
                            v2cast(n)

            @blk.tensor
            def _(t):
                for pm in range(8):
                    if pm >= 2:
                        t.wait_ge(sm["fpev"], b["fpev"] + pm - 1)
                    for kc in range(8):
                        n = pm * 8 + kc
                        t.wait_ge(sm["fu1c"], b["fu1c"] + n + 1)
                        for j in range(4):
                            k = kc * 4 + j
                            for h2 in range(2):
                                last = t.matmul(q4[2 * (pm % 2) + h2][:, :],
                                                u1bf[n % 2][:, j * 128:(j + 1) * 128],
                                                y_sb[:, k * 1024 + h2 * 512:k * 1024 + (h2 + 1) * 512],
                                                start=(k == 0), stop=(k == 31))
                        sm.inc(last, "fu1u")
                for i in range(NI + 1):
                    if i < NI:
                        t.wait_ge(sm["fwc"], b["fwc"] + 2 * (i + 1))
                        t.wait_ge(sm["fhdr"], b["fhdr"] + i)
                        if i == 0:
                            t.wait_ge(sm["fpev"], b["fpev"] + 8)
                        for part in range(2):
                            for k in range(8):
                                for h2 in range(2):
                                    cw = part * 1024 + k * 128
                                    last = t.matmul(
                                        q4[2 * part + h2][:, :],
                                        wbf[i % 2][:, cw:cw + 128],
                                        p_sb[:, k * 1024 + h2 * 512:k * 1024 + (h2 + 1) * 512],
                                        start=(k == 0), stop=(k == 7))
                        sm.inc(last, "fhmm")
                    if i >= 1:
                        ii = i - 1
                        t.wait_ge(sm["fact"], b["fact"] + ii + 1)
                        for zm in range(8):
                            g = ii * 8 + zm
                            if g >= 2:
                                t.wait_ge(sm["fzd"], b["fzd"] + g - 1)
                            for h2 in range(2):
                                last = t.matmul(
                                    zp[g % 2][:, h2 * 512:(h2 + 1) * 512],
                                    wbf[ii % 2][:, 2048 + zm * 128:2048 + (zm + 1) * 128],
                                    act_bf[ii % 2][:, h2 * 512:(h2 + 1) * 512],
                                    start=True, stop=True)
                            sm.inc(last, "fzmm")
                for om in range(32):
                    t.wait_ge(sm["fv2c"], b["fv2c"] + 2 * (om + 1))
                    if om == 0:
                        t.wait_ge(sm["fzcast"], b["fzcast"] + 1)
                        t.wait_ge(sm["fhdr"], b["fhdr"] + NI)
                    if om >= 2:
                        t.wait_ge(sm["ft"], b["ft"] + om - 1)
                    for k in range(8):
                        for h2 in range(2):
                            last = t.matmul(q4[2 * (om % 2) + h2][:, :],
                                            v2bf[om % 2][:, k * 128:(k + 1) * 128],
                                            p_sb[:, k * 1024 + h2 * 512:k * 1024 + (h2 + 1) * 512],
                                            start=(k == 0), stop=(k == 7))
                    sm.inc(last, "fomm")

            @blk.vector
            def _(v):
                for i in range(NI):
                    if i >= 1:
                        ii = i - 1
                        for zm in range(8):
                            g = ii * 8 + zm
                            v.wait_ge(sm["fzmm"], b["fzmm"] + g + 1)
                            dst = zacc[:, zm * 1024:(zm + 1) * 1024]
                            if ii == 0:
                                sm.inc(v.tensor_copy(dst, zp[g % 2][:, :]), "fzd")
                            else:
                                sm.inc(v.tensor_add(dst, dst, zp[g % 2][:, :]), "fzd")
                    v.wait_ge(sm["fhdr"], b["fhdr"] + i + 1)
                    if i >= 2:
                        v.wait_ge(sm["fzmm"], b["fzmm"] + 8 * (i - 1))
                    sm.inc(v.tensor_mul(act_bf[i % 2][:, :], sg_sb[:, :], ub_sb[:, :]),
                           "fact")
                ii = NI - 1
                for zm in range(8):
                    g = ii * 8 + zm
                    v.wait_ge(sm["fzmm"], b["fzmm"] + g + 1)
                    dst = zacc[:, zm * 1024:(zm + 1) * 1024]
                    sm.inc(v.tensor_add(dst, dst, zp[g % 2][:, :]), "fzd")
                for om in range(32):
                    v.wait_ge(sm["ft"], b["ft"] + om + 1)
                    v.wait_ge(sm[f"fxld{om % 2}"],
                              b[f"fxld{om % 2}"] + 16 * (om // 2 + 1))
                    sm.inc(v.tensor_add(xst2[om % 2][:, :], xst2[om % 2][:, :],
                                        t_sb[om % 2][:, :]), "fxadd")




def build_A():
    nc = bass.Bass("TRN2", target_bir_lowering=False)
    X = nc.dram_tensor("x", [NPT, 128, TOK], F32, kind="ExternalInput")
    U1P = nc.dram_tensor("u1p", [8, 128, 4096], F32, kind="ExternalInput")
    V1U2P = nc.dram_tensor("v1u2p", [86, 128, 3072], F32, kind="ExternalInput")
    V2P = nc.dram_tensor("v2p", [32, 128, 1024], F32, kind="ExternalInput")
    WKVP = nc.dram_tensor("wkvp", [128, 8192], F32, kind="ExternalInput")
    B1P = nc.dram_tensor("b1p", [128, 172], F32, kind="ExternalInput")
    B2P = nc.dram_tensor("b2p", [128, 32], F32, kind="ExternalInput")
    ONESBF = nc.dram_tensor("ones_bf", [128, 1], BF16, kind="ExternalInput")
    ONESF = nc.dram_tensor("ones_f32r", [1, 128], F32, kind="ExternalInput")
    X1 = nc.dram_tensor("x1", [NPT, 128, TOK], F32, kind="ExternalOutput")
    Y2 = nc.dram_tensor("y2", [NPT, 128, TOK], BF16, kind="ExternalOutput")
    KV = nc.dram_tensor("kv", [2, 128, TOK], BF16, kind="ExternalOutput")
    wd = {"u1p": U1P, "v1u2p": V1U2P, "v2p": V2P}

    with ExitStack() as es:
        sb = lambda n, sh, dt: es.enter_context(nc.sbuf_tensor(n, sh, dt))
        y_sb = sb("y", [128, NPT * 1024], BF16)
        p_sb = sb("p", [128, 8 * 1024], BF16)
        zacc = sb("zacc", [128, 8 * 1024], F32)
        bufs = _common_bufs(nc, es)
        kvw = bufs["u1st"]
        kvwb = bufs["u1bf"]
        kv_bf = bufs["act_bf"]
        sems = Sems(nc, es, ["init"] + LN_SEMS + FFN_SEMS +
                    ["kvld0", "kvld1", "kvc", "kvmm", "kvev", "yst"])

        with nc.Block() as blk:
            @blk.gpsimd
            def _(g):
                for dst, src in ((bufs["b1p"], B1P), (bufs["b2h"], B2P),
                                 (bufs["ones_bf"], ONESBF), (bufs["ones_f32r"], ONESF)):
                    sems.inc(g.dma_start(out=dst[:, :], in_=src[:, :]), "init", 16)

            @blk.vector
            def _(v):
                v.wait_ge(sems["init"], 64)
                v.tensor_scalar(bufs["b2h"][:, :], bufs["b2h"][:, :], 0.5, None,
                                ALU.mult)

        _emit_ln(nc, sems, bufs, X, y_sb)
        _emit_ffn(nc, sems, bufs, y_sb, p_sb, zacc, X, X1, wd, is_final=False)
        _emit_ln(nc, sems, bufs, X1, y_sb,
                 pre=[("fst0", sems.n["fst0"]), ("fst1", sems.n["fst1"])])

        with ExitStack() as pse:
            kvp = [pse.enter_context(nc.psum_tensor(f"kvp{i}", [128, 512], F32))
                   for i in range(4)]
            with nc.Block() as blk:
                @blk.gpsimd
                def _(g):
                    for pt in range(NPT):
                        sems.inc(g.dma_start(out=Y2[pt, :, :],
                                             in_=y_sb[:, pt * 1024:(pt + 1) * 1024]),
                                 "yst", 16)

                @blk.sync
                def _(s):
                    for k in range(32):
                        if k >= 2:
                            s.wait_ge(sems["kvc"], k - 1)
                        sems.inc(s.dma_start(out=kvw[k % 2][:, 0:256],
                                             in_=WKVP[:, k * 256:(k + 1) * 256]),
                                 f"kvld{k % 2}", 16)

                @blk.scalar
                def _(s):
                    for k in range(32):
                        s.wait_ge(sems[f"kvld{k % 2}"], 16 * (k // 2 + 1))
                        if k >= 2:
                            s.wait_ge(sems["kvmm"], k - 1)
                        sems.inc(s.activation(kvwb[k % 2][:, 0:256],
                                              kvw[k % 2][:, 0:256], AF.Copy), "kvc")
                    s.wait_ge(sems["kvmm"], 32)
                    for m2 in range(2):
                        for h2 in range(2):
                            last = s.activation(kv_bf[m2][:, h2 * 512:(h2 + 1) * 512],
                                                kvp[2 * m2 + h2][:, :], AF.Copy)
                    sems.inc(last, "kvev")

                @blk.tensor
                def _(t):
                    for k in range(32):
                        t.wait_ge(sems["kvc"], k + 1)
                        for m2 in range(2):
                            for h2 in range(2):
                                last = t.matmul(
                                    kvp[2 * m2 + h2][:, :],
                                    kvwb[k % 2][:, m2 * 128:(m2 + 1) * 128],
                                    y_sb[:, k * 1024 + h2 * 512:k * 1024 + (h2 + 1) * 512],
                                    start=(k == 0), stop=(k == 31))
                        sems.inc(last, "kvmm")

                @blk.gpsimd
                def _(g):
                    g.wait_ge(sems["kvev"], 1)
                    for m2 in range(2):
                        sems.inc(g.dma_start(out=KV[m2, :, :], in_=kv_bf[m2][:, :]),
                                 "yst", 16)
                    g.wait_ge(sems["yst"], 16 * (NPT + 2))
                    g.wait_ge(sems["fst0"], 16 * 16)
                    g.wait_ge(sems["fst1"], 16 * 16)
    return nc




def build_B():
    nc = bass.Bass("TRN2", target_bir_lowering=False)
    X1 = nc.dram_tensor("x1", [NPT, 128, TOK], F32, kind="ExternalInput")
    Y2D = nc.dram_tensor("y2", [NPT, 128, TOK], BF16, kind="ExternalInput")
    KVG = nc.dram_tensor("kvg", [4, 2, 128, TOK], BF16, kind="ExternalInput")
    WQP = nc.dram_tensor("wqp", [16, 128, 8192], F32, kind="ExternalInput")
    WOP = nc.dram_tensor("wop", [16, 128, 8192], F32, kind="ExternalInput")
    U1P = nc.dram_tensor("u1p", [8, 128, 4096], F32, kind="ExternalInput")
    V1U2P = nc.dram_tensor("v1u2p", [86, 128, 3072], F32, kind="ExternalInput")
    V2P = nc.dram_tensor("v2p", [32, 128, 1024], F32, kind="ExternalInput")
    B1P = nc.dram_tensor("b1p", [128, 172], F32, kind="ExternalInput")
    B2P = nc.dram_tensor("b2p", [128, 32], F32, kind="ExternalInput")
    R0D = nc.dram_tensor("r0", [128, 512], F32, kind="ExternalInput")
    MK1 = nc.dram_tensor("mask1", [128, 512], F32, kind="ExternalInput")
    OTAB = nc.dram_tensor("otab", [128, NH * N_SC], F32, kind="ExternalInput")
    IDNB = nc.dram_tensor("idn_bf", [128, 128], BF16, kind="ExternalInput")
    IDNF = nc.dram_tensor("idn_f32", [128, 128], F32, kind="ExternalInput")
    ONESBF = nc.dram_tensor("ones_bf", [128, 1], BF16, kind="ExternalInput")
    ONESF = nc.dram_tensor("ones_f32r", [1, 128], F32, kind="ExternalInput")
    OUT = nc.dram_tensor("out", [NPT, 128, TOK], F32, kind="ExternalOutput")
    XB = nc.dram_tensor("xbuf", [NPT, 128, TOK], F32, kind="Internal")
    wd = {"u1p": U1P, "v1u2p": V1U2P, "v2p": V2P}

    with ExitStack() as es0:
        sb0 = lambda n, sh, dt: es0.enter_context(nc.sbuf_tensor(n, sh, dt))
        b1p_sb = sb0("b1ps", [128, 172], F32)
        b2h_sb = sb0("b2h", [128, 32], F32)
        onesbf_sb = sb0("onesbf", [128, 1], BF16)
        onesf_sb = sb0("onesf", [1, 128], F32)
        sems = Sems(nc, es0, ["init"] + LN_SEMS + FFN_SEMS + [
            "vktr", "vkev", "knsq", "knmm", "knev",
            "qwld0", "qwld1", "qwc", "qwu", "qev",
            "qnsq", "qnmm", "qnev", "bch",
            "scmm", "scv", "sexp", "strc", "ptev", "spv",
            "nsum", "nrow1", "nrec", "nrb2", "nrbc", "ndone", "relw",
            "wwld0", "wwld1", "wwc", "wwu",
            "wxld0", "wxld1", "wxadd", "wst20", "wst21",
        ])

        with ExitStack() as esA:
            sbA = lambda n, sh, dt: esA.enter_context(nc.sbuf_tensor(n, sh, dt))
            r0_sb = sbA("r0s", [128, 512], F32)
            mk_sb = sbA("mk1s", [128, 512], F32)
            ot_sb = sbA("ots", [128, NH * N_SC], F32)
            ba_sb = sbA("bas", [128, NH * N_SC], F32)
            idnb_sb = sbA("idnb", [128, 128], BF16)
            idnf_sb = sbA("idnf", [128, 128], F32)
            relh_sb = sbA("relh", [128, 512], F32)
            rm_sb = sbA("relm", [128, 512], F32)
            sc2_sb = [sbA(f"sc2{i}", [128, 512], F32) for i in range(2)]
            pr_sb = [sbA(f"pr{i}", [128, 512], BF16) for i in range(4)]
            pt_sb = sbA("pts", [128, 32 * 128], BF16)
            sums_sb = sbA("sumss", [128, 2 * N_SC], F32)
            ssum_sb = sbA("ssum", [128, 1], F32)
            srow_sb = sbA("srow", [1, 128], F32)
            rb_sb = sbA("rbs", [128, 128], F32)
            qsq_sb = sbA("qsqs", [128, 1024], BF16)
            nrow_sb = sbA("nrow", [1, 1024], F32)
            bsc_sb = sbA("bsc", [1, 96], F32)
            bcol_sb = sbA("bcol", [128, 32], F32)
            wqs = [sbA(f"wqs{i}", [128, 256], F32) for i in range(2)]
            wqb = [sbA(f"wqb{i}", [128, 256], BF16) for i in range(2)]
            kt_sb = sbA("kt", [128, 4096], BF16)
            vk_sb = sbA("vk", [128, 4096], BF16)
            y_sb = sbA("y", [128, NPT * 1024], BF16)
            q_sb = sbA("q", [128, NPT * 1024], BF16)

            with ExitStack() as esV:
                vt_sb = esV.enter_context(nc.sbuf_tensor("vt", [128, 4096], BF16))
                tp = [esV.enter_context(nc.psum_tensor(f"vtp{i}", [128, 128], BF16))
                      for i in range(2)]
                knp = esV.enter_context(nc.psum_tensor("knp", [1, 512], F32))

                with nc.Block() as blk:
                    @blk.gpsimd
                    def _(g):
                        for dst, src in ((b1p_sb, B1P), (b2h_sb, B2P),
                                         (onesbf_sb, ONESBF), (onesf_sb, ONESF),
                                         (r0_sb, R0D), (mk_sb, MK1), (ot_sb, OTAB),
                                         (idnb_sb, IDNB), (idnf_sb, IDNF)):
                            sems.inc(g.dma_start(out=dst[:, :], in_=src[:, :]),
                                     "init", 16)
                        for r in range(4):
                            sems.inc(g.dma_start(out=kt_sb[:, r * 1024:(r + 1) * 1024],
                                                 in_=KVG[r, 0, :, :]), "init", 16)
                            sems.inc(g.dma_start(out=vt_sb[:, r * 1024:(r + 1) * 1024],
                                                 in_=KVG[r, 1, :, :]), "init", 16)
                        for pt in range(NPT):
                            sems.inc(g.dma_start(out=y_sb[:, pt * 1024:(pt + 1) * 1024],
                                                 in_=Y2D[pt, :, :]), "init", 16)

                    @blk.vector
                    def _(v):
                        v.wait_ge(sems["init"], sems.n["init"])
                        v.tensor_scalar(b2h_sb[:, :], b2h_sb[:, :], 0.5, None,
                                        ALU.mult)

                order = [(r, i) for r in range(4) for i in range(8)]
                with nc.Block() as blk:
                    @blk.tensor
                    def _(t):
                        for n, (r, i) in enumerate(order):
                            if n >= 2:
                                t.wait_ge(sems["vkev"], n - 1)
                            sems.inc(t.transpose(
                                tp[n % 2][:, :],
                                vt_sb[:, r * 1024 + i * 128:r * 1024 + (i + 1) * 128],
                                idnb_sb[:, :]), "vktr")
                        for ch in range(4):
                            t.wait_ge(sems["knsq"], ch + 1)
                            for h2 in range(2):
                                n = 2 * ch + h2
                                if h2 == 1:
                                    t.wait_ge(sems["knev"], 3 * ch + 1)
                                elif ch >= 1:
                                    t.wait_ge(sems["knev"], 3 * ch - 1)
                                sems.inc(t.matmul(knp[:, :], onesbf_sb[:, :],
                                                  qsq_sb[:, h2 * 512:(h2 + 1) * 512],
                                                  start=True, stop=True), "knmm")

                    @blk.vector
                    def _(v):
                        for n, (r, i) in enumerate(order):
                            kt = 4 * i + r
                            v.wait_ge(sems["vktr"], n + 1)
                            sems.inc(v.tensor_copy(vk_sb[:, kt * 128:(kt + 1) * 128],
                                                   tp[n % 2][:, :]), "vkev")
                        for ch in range(4):
                            for h2 in range(2):
                                n = 2 * ch + h2
                                v.wait_ge(sems["knmm"], n + 1)
                                if h2 == 0 and ch >= 1:
                                    v.wait_ge(sems["knev"], 3 * ch)
                                sems.inc(v.tensor_copy(
                                    nrow_sb[:, h2 * 512:(h2 + 1) * 512], knp[:, :]),
                                    "knev")
                            v.wait_ge(sems["knev"], 3 * ch + 2)
                            sems.inc(v.reduce_max(bsc_sb[:, 64 + ch:65 + ch],
                                                  nrow_sb[:, 0:1024], axis=AX), "knev")
                        v.wait_ge(sems["knev"], 12)
                        sems.inc(v.reduce_max(bsc_sb[:, 0:1], bsc_sb[:, 64:68],
                                              axis=AX), "knev")

                    @blk.scalar
                    def _(s):
                        for ch in range(4):
                            if ch >= 1:
                                s.wait_ge(sems["knmm"], 2 * ch)
                            sems.inc(s.activation(qsq_sb[:, :],
                                                  kt_sb[:, ch * 1024:(ch + 1) * 1024],
                                                  AF.Square), "knsq")

            with ExitStack() as esQ:
                qp = [esQ.enter_context(nc.psum_tensor(f"qp{i}", [128, 512], F32))
                      for i in range(8)]
                with nc.Block() as blk:
                    @blk.sync
                    def _(s):
                        for g in range(16):
                            for k in range(32):
                                n = g * 32 + k
                                if n >= 2:
                                    s.wait_ge(sems["qwc"], n - 1)
                                sems.inc(s.dma_start(
                                    out=wqs[n % 2][:, :],
                                    in_=WQP[g, :, k * 256:(k + 1) * 256]),
                                    f"qwld{n % 2}", 16)

                    @blk.scalar
                    def _(s):
                        for g in range(16):
                            for k in range(32):
                                n = g * 32 + k
                                s.wait_ge(sems[f"qwld{n % 2}"], 16 * (n // 2 + 1))
                                if n >= 2:
                                    s.wait_ge(sems["qwu"], n - 1)
                                sems.inc(s.activation(wqb[n % 2][:, :], wqs[n % 2][:, :],
                                                      AF.Copy), "qwc")
                            s.wait_ge(sems["qwu"], 32 * (g + 1))
                            for m2 in range(2):
                                for h2 in range(2):
                                    hh = 2 * g + m2
                                    c0 = hh * 1024 + h2 * 512
                                    last = s.activation(
                                        q_sb[:, c0:c0 + 512],
                                        qp[4 * (g % 2) + 2 * m2 + h2][:, :], AF.Copy,
                                        scale=SCALE)
                            sems.inc(last, "qev")

                    @blk.tensor
                    def _(t):
                        for g in range(16):
                            if g >= 2:
                                t.wait_ge(sems["qev"], g - 1)
                            for k in range(32):
                                n = g * 32 + k
                                t.wait_ge(sems["qwc"], n + 1)
                                for m2 in range(2):
                                    for h2 in range(2):
                                        last = t.matmul(
                                            qp[4 * (g % 2) + 2 * m2 + h2][:, :],
                                            wqb[n % 2][:, m2 * 128:(m2 + 1) * 128],
                                            y_sb[:, k * 1024 + h2 * 512:k * 1024 + (h2 + 1) * 512],
                                            start=(k == 0), stop=(k == 31))
                                sems.inc(last, "qwu")

            with ExitStack() as esN:
                np1 = esN.enter_context(nc.psum_tensor("np1", [1, 512], F32))
                bbp = esN.enter_context(nc.psum_tensor("bbp", [128, 32], F32))
                with nc.Block() as blk:
                    @blk.scalar
                    def _(s):
                        for h in range(NH):
                            if h >= 1:
                                s.wait_ge(sems["qnmm"], 2 * h)
                            sems.inc(s.activation(qsq_sb[:, :],
                                                  q_sb[:, h * 1024:(h + 1) * 1024],
                                                  AF.Square), "qnsq")
                        s.wait_ge(sems["bch"], 1)
                        sems.inc(s.activation(bsc_sb[:, 32:64], bsc_sb[:, 32:64],
                                              AF.Sqrt), "bch")

                    @blk.tensor
                    def _(t):
                        for h in range(NH):
                            t.wait_ge(sems["qnsq"], h + 1)
                            for h2 in range(2):
                                if h2 == 1:
                                    t.wait_ge(sems["qnev"], 3 * h + 1)
                                elif h >= 1:
                                    t.wait_ge(sems["qnev"], 3 * h - 1)
                                sems.inc(t.matmul(np1[:, :], onesbf_sb[:, :],
                                                  qsq_sb[:, h2 * 512:(h2 + 1) * 512],
                                                  start=True, stop=True), "qnmm")
                        t.wait_ge(sems["bch"], 2)
                        sems.inc(t.matmul(bbp[:, :], onesf_sb[:, :], bsc_sb[:, 32:64],
                                          start=True, stop=True), "qnmm")

                    @blk.vector
                    def _(v):
                        for h in range(NH):
                            for h2 in range(2):
                                n = 2 * h + h2
                                v.wait_ge(sems["qnmm"], n + 1)
                                if h2 == 0 and h >= 1:
                                    v.wait_ge(sems["qnev"], 3 * h)
                                sems.inc(v.tensor_copy(
                                    nrow_sb[:, h2 * 512:(h2 + 1) * 512], np1[:, :]),
                                    "qnev")
                            v.wait_ge(sems["qnev"], 3 * h + 2)
                            sems.inc(v.reduce_max(bsc_sb[:, 1 + h:2 + h],
                                                  nrow_sb[:, 0:1024], axis=AX), "qnev")
                        v.wait_ge(sems["qnev"], 3 * NH)
                        sems.inc(v.tensor_scalar(bsc_sb[:, 32:64], bsc_sb[:, 1:33],
                                                 bsc_sb[:, 0:1], None, ALU.mult),
                                 "bch")

                    @blk.scalar
                    def _(s):
                        s.wait_ge(sems["qnmm"], 2 * NH + 1)
                        sems.inc(s.activation(bcol_sb[:, :], bbp[:, :], AF.Copy),
                                 "bch")

                    @blk.vector
                    def _(v):
                        v.wait_ge(sems["bch"], 3)
                        for h in range(NH):
                            sems.inc(v.tensor_scalar(ba_sb[:, h * N_SC:(h + 1) * N_SC],
                                                     ot_sb[:, h * N_SC:(h + 1) * N_SC],
                                                     bcol_sb[:, h:h + 1], None,
                                                     ALU.subtract), "bch")

            chunks = []
            slot_of = {}
            g = 0
            for h in range(NH):
                for i in range(8):
                    for c in range(i + 1):
                        chunks.append((h, i, c, g))
                        if c == i:
                            slot_of[g] = (h, i, h * 8 + i)
                        g += 1
            NG = g

            with ExitStack() as esE:
                scp = [esE.enter_context(nc.psum_tensor(f"scp{i}", [128, 512], F32))
                       for i in range(2)]
                ptp = [esE.enter_context(nc.psum_tensor(f"ptp{i}", [128, 512], BF16))
                       for i in range(2)]
                pvp = [esE.enter_context(nc.psum_tensor(f"pvp{i}", [128, 128], F32))
                       for i in range(2)]
                nrp = esE.enter_context(nc.psum_tensor("nrp", [1, 128], F32))
                rbp = esE.enter_context(nc.psum_tensor("rbp", [128, 128], F32))
                with nc.Block() as blk:
                    b_scmm = sems.n["scmm"]; b_scv = sems.n["scv"]
                    b_sexp = sems.n["sexp"]; b_strc = sems.n["strc"]
                    b_ptev = sems.n["ptev"]; b_spv = sems.n["spv"]
                    b_nsum = sems.n["nsum"]; b_nrow1 = sems.n["nrow1"]
                    b_nrec = sems.n["nrec"]; b_nrb2 = sems.n["nrb2"]
                    b_nrbc = sems.n["nrbc"]; b_ndone = sems.n["ndone"]

                    def kt_rhs(c):
                        return bass.AP(kt_sb, c * 128,
                                       [[4096, 128], [1024, 4], [1, 128]])

                    @blk.tensor
                    def _(t):
                        for gg in range(NG + 6):
                            if gg < NG:
                                h, i, c, _ = chunks[gg]
                                if gg >= 2:
                                    t.wait_ge(sems["scv"], b_scv + gg - 1)
                                sems.inc(t.matmul(
                                    scp[gg % 2][:, :],
                                    q_sb[:, h * 1024 + i * 128:h * 1024 + (i + 1) * 128],
                                    kt_rhs(c), start=True, stop=True), "scmm")
                            gp = gg - 2
                            if 0 <= gp < NG:
                                t.wait_ge(sems["sexp"], b_sexp + gp + 1)
                                if gp >= 2:
                                    t.wait_ge(sems["ptev"], b_ptev + gp - 1)
                                for bb in range(4):
                                    last = t.transpose(
                                        ptp[gp % 2][:, bb * 128:(bb + 1) * 128],
                                        pr_sb[gp % 4][:, bb * 128:(bb + 1) * 128],
                                        idnb_sb[:, :])
                                sems.inc(last, "strc")
                                if gp in slot_of:
                                    h2_, i2_, sl = slot_of[gp]
                                    t.wait_ge(sems["ptev"], b_ptev + gp + 1)
                                    if sl >= 2:
                                        t.wait_ge(sems["ndone"], b_ndone + sl - 1)
                                    nkt = 4 * (i2_ + 1)
                                    for kt in range(nkt):
                                        last = t.matmul(
                                            pvp[sl % 2][:, :],
                                            vk_sb[:, kt * 128:(kt + 1) * 128],
                                            pt_sb[:, kt * 128:(kt + 1) * 128],
                                            start=(kt == 0), stop=(kt == nkt - 1))
                                    sems.inc(last, "spv")
                                    t.wait_ge(sems["nsum"], b_nsum + sl + 1)
                                    if sl >= 1:
                                        t.wait_ge(sems["nrec"], b_nrec + sl)
                                    sems.inc(t.matmul(nrp[:, :], ssum_sb[:, :],
                                                      idnf_sb[:, :], start=True,
                                                      stop=True), "nrow1")
                            gm = gg - 3
                            if 0 <= gm < NG and gm in slot_of:
                                _, _, sl = slot_of[gm]
                                t.wait_ge(sems["nrec"], b_nrec + sl + 1)
                                if sl >= 1:
                                    t.wait_ge(sems["nrbc"], b_nrbc + sl)
                                sems.inc(t.matmul(rbp[:, :], onesf_sb[:, :],
                                                  srow_sb[:, :], start=True,
                                                  stop=True), "nrb2")

                    @blk.vector
                    def _(v):
                        for gg in range(NG + 6):
                            if gg < NG:
                                h, i, c, _ = chunks[gg]
                                if i == 0 and c == 0:
                                    if h >= 1:
                                        v.wait_ge(sems["scv"], b_scv + N_SC * h)
                                    v.scalar_tensor_tensor(rm_sb[:, :], r0_sb[:, :],
                                                           float(SLOPES[h]),
                                                           mk_sb[:, :], ALU.mult,
                                                           ALU.add)
                                    sems.inc(v.tensor_scalar(relh_sb[:, :], r0_sb[:, :],
                                                             float(SLOPES[h]), None,
                                                             ALU.mult), "relw")
                                    v.wait_ge(sems["relw"], h + 1)
                                v.wait_ge(sems["scmm"], b_scmm + gg + 1)
                                if gg >= 2:
                                    v.wait_ge(sems["sexp"], b_sexp + gg - 1)
                                rel = rm_sb if c == i else relh_sb
                                sems.inc(v.scalar_tensor_tensor(
                                    sc2_sb[gg % 2][:, :], scp[gg % 2][:, :], 1.0,
                                    rel[:, :], ALU.mult, ALU.add), "scv")
                            gp = gg - 2
                            if 0 <= gp < NG:
                                h2_, i2_, c2_, _ = chunks[gp]
                                v.wait_ge(sems["strc"], b_strc + gp + 1)
                                if c2_ == 0 and (h2_ * 8 + i2_) >= 1:
                                    v.wait_ge(sems["spv"], b_spv + h2_ * 8 + i2_)
                                sems.inc(v.tensor_copy(
                                    pt_sb[:, c2_ * 512:(c2_ + 1) * 512],
                                    ptp[gp % 2][:, :]), "ptev")
                                if gp in slot_of:
                                    _, _, sl = slot_of[gp]
                                    v.wait_ge(sems["sexp"], b_sexp + gp + 1)
                                    if sl >= 1:
                                        v.wait_ge(sems["nrow1"], b_nrow1 + sl)
                                    i0 = (h2_ % 2) * N_SC + _scidx(i2_, 0)
                                    sems.inc(v.reduce_sum(ssum_sb[:, :],
                                                          sums_sb[:, i0:i0 + i2_ + 1],
                                                          axis=AX), "nsum")
                            gr = gg - 3
                            if 0 <= gr < NG and gr in slot_of:
                                _, _, sl = slot_of[gr]
                                v.wait_ge(sems["nrow1"], b_nrow1 + sl + 1)
                                sems.inc(v.reciprocal(srow_sb[:, :], nrp[:, :]),
                                         "nrec")
                            gf = gg - 5
                            if 0 <= gf < NG and gf in slot_of:
                                h2_, i2_, sl = slot_of[gf]
                                v.wait_ge(sems["nrbc"], b_nrbc + sl + 1)
                                v.wait_ge(sems["spv"], b_spv + sl + 1)
                                c0 = h2_ * 1024 + i2_ * 128
                                sems.inc(v.tensor_mul(q_sb[:, c0:c0 + 128],
                                                      pvp[sl % 2][:, :], rb_sb[:, :]),
                                         "ndone")

                    @blk.scalar
                    def _(s):
                        for gg in range(NG + 6):
                            if gg < NG:
                                h, i, c, _ = chunks[gg]
                                if i == 0 and c == 0 and h >= 2:
                                    s.wait_ge(sems["nsum"], b_nsum + 8 * (h - 1))
                                s.wait_ge(sems["scv"], b_scv + gg + 1)
                                if gg >= 4:
                                    s.wait_ge(sems["strc"], b_strc + gg - 3)
                                col = h * N_SC + _scidx(i, c)
                                scol = (h % 2) * N_SC + _scidx(i, c)
                                sems.inc(s.activation(
                                    pr_sb[gg % 4][:, :], sc2_sb[gg % 2][:, :], AF.Exp,
                                    bias=ba_sb[:, col:col + 1],
                                    accum_out=sums_sb[:, scol:scol + 1]), "sexp")
                            gb = gg - 4
                            if 0 <= gb < NG and gb in slot_of:
                                _, _, sl = slot_of[gb]
                                s.wait_ge(sems["nrb2"], b_nrb2 + sl + 1)
                                if sl >= 1:
                                    s.wait_ge(sems["ndone"], b_ndone + sl)
                                sems.inc(s.activation(rb_sb[:, :], rbp[:, :], AF.Copy),
                                         "nrbc")

            with ExitStack() as esW:
                xsw = [esW.enter_context(nc.sbuf_tensor(f"xsw{i}", [128, 1024], F32))
                       for i in range(2)]
                op = [esW.enter_context(nc.psum_tensor(f"wop{i}", [128, 512], F32))
                      for i in range(8)]
                with nc.Block() as blk:
                    @blk.sync
                    def _(s):
                        for g in range(16):
                            for k in range(32):
                                n = g * 32 + k
                                if n >= 2:
                                    s.wait_ge(sems["wwc"], n - 1)
                                sems.inc(s.dma_start(
                                    out=wqs[n % 2][:, :],
                                    in_=WOP[g, :, k * 256:(k + 1) * 256]),
                                    f"wwld{n % 2}", 16)

                    @blk.scalar
                    def _(s):
                        for g in range(16):
                            for k in range(32):
                                n = g * 32 + k
                                s.wait_ge(sems[f"wwld{n % 2}"], 16 * (n // 2 + 1))
                                if n >= 2:
                                    s.wait_ge(sems["wwu"], n - 1)
                                sems.inc(s.activation(wqb[n % 2][:, :],
                                                      wqs[n % 2][:, :], AF.Copy),
                                         "wwc")

                    @blk.tensor
                    def _(t):
                        for g in range(16):
                            if g >= 1:
                                t.wait_ge(sems["wxadd"], 2 * (g - 1))
                            for k in range(32):
                                n = g * 32 + k
                                t.wait_ge(sems["wwc"], n + 1)
                                for m2 in range(2):
                                    for h2 in range(2):
                                        last = t.matmul(
                                            op[4 * (g % 2) + 2 * m2 + h2][:, :],
                                            wqb[n % 2][:, m2 * 128:(m2 + 1) * 128],
                                            q_sb[:, k * 1024 + h2 * 512:k * 1024 + (h2 + 1) * 512],
                                            start=(k == 0), stop=(k == 31))
                                sems.inc(last, "wwu")

                    @blk.gpsimd
                    def _(g):
                        for om in range(32):
                            if om >= 2:
                                g.wait_ge(sems[f"wst2{om % 2}"], 16 * (om // 2))
                            sems.inc(g.dma_start(out=xsw[om % 2][:, :],
                                                 in_=X1[om, :, :]),
                                     f"wxld{om % 2}", 16)
                            g.wait_ge(sems["wxadd"], om + 1)
                            sems.inc(g.dma_start(out=XB[om, :, :],
                                                 in_=xsw[om % 2][:, :]),
                                     f"wst2{om % 2}", 16)

                    @blk.vector
                    def _(v):
                        for om in range(32):
                            g2, m2 = om // 2, om % 2
                            v.wait_ge(sems["wwu"], 32 * (g2 + 1))
                            v.wait_ge(sems[f"wxld{om % 2}"], 16 * (om // 2 + 1))
                            for h2 in range(2):
                                last = v.tensor_add(
                                    xsw[om % 2][:, h2 * 512:(h2 + 1) * 512],
                                    xsw[om % 2][:, h2 * 512:(h2 + 1) * 512],
                                    op[4 * (g2 % 2) + 2 * m2 + h2][:, :])
                            sems.inc(last, "wxadd")

        with ExitStack() as esF:
            sbF = lambda n, sh, dt: esF.enter_context(nc.sbuf_tensor(n, sh, dt))
            y2_sb = sbF("y3", [128, NPT * 1024], BF16)
            p_sb = sbF("p", [128, 8 * 1024], BF16)
            zacc = sbF("zacc", [128, 8 * 1024], F32)
            bufs = _common_bufs(nc, esF, pfx="f_")
            bufs["b1p"] = b1p_sb
            bufs["b2h"] = b2h_sb
            bufs["ones_bf"] = onesbf_sb
            bufs["ones_f32r"] = onesf_sb
            _emit_ln(nc, sems, bufs, XB, y2_sb,
                     pre=[("wst20", sems.n["wst20"]), ("wst21", sems.n["wst21"])])
            _emit_ffn(nc, sems, bufs, y2_sb, p_sb, zacc, XB, OUT, wd, is_final=True)
    return nc



_CACHE = {}


def kernel(L0_attn_hidden_states, L0_attn_Wq, L0_attn_Wkv, L0_attn_Wo,
           L0_ffn_U1, L0_ffn_V1, L0_ffn_U2, L0_ffn_V2, L0_ffn_b1, L0_ffn_b2):
    x = np.asarray(L0_attn_hidden_states, dtype=np.float32)
    wp = _pack_weights(np.asarray(L0_attn_Wq, np.float32),
                       np.asarray(L0_attn_Wkv, np.float32),
                       np.asarray(L0_attn_Wo, np.float32),
                       np.asarray(L0_ffn_U1, np.float32),
                       np.asarray(L0_ffn_V1, np.float32),
                       np.asarray(L0_ffn_U2, np.float32),
                       np.asarray(L0_ffn_V2, np.float32),
                       np.asarray(L0_ffn_b1, np.float32),
                       np.asarray(L0_ffn_b2, np.float32))
    consts = [_consts_for_core(j) for j in range(4)]
    rows = [_core_token_rows(j) for j in range(4)]

    if "A" not in _CACHE:
        _CACHE["A"] = build_A()
    if "B" not in _CACHE:
        _CACHE["B"] = build_B()

    in_A = []
    for c in range(8):
        b, j = c // 4, c % 4
        xc = np.ascontiguousarray(x[b][rows[j], :].T.reshape(NPT, 128, TOK))
        in_A.append({"x": xc, "u1p": wp["u1p"], "v1u2p": wp["v1u2p"],
                     "v2p": wp["v2p"], "wkvp": wp["wkvp"], "b1p": wp["b1p"],
                     "b2p": wp["b2p"], "ones_bf": consts[j]["ones_bf"],
                     "ones_f32r": consts[j]["ones_f32r"]})
    resA = run_bass_kernel_spmd(_CACHE["A"], in_A, core_ids=list(range(8)))

    in_B = []
    for c in range(8):
        b, j = c // 4, c % 4
        kvg = np.stack([np.asarray(resA.results[b * 4 + r]["kv"]) for r in range(4)])
        cc = consts[j]
        in_B.append({"x1": np.asarray(resA.results[c]["x1"]),
                     "y2": np.asarray(resA.results[c]["y2"]),
                     "kvg": np.ascontiguousarray(kvg),
                     "wqp": wp["wqp"], "wop": wp["wop"], "u1p": wp["u1p"],
                     "v1u2p": wp["v1u2p"], "v2p": wp["v2p"], "b1p": wp["b1p"],
                     "b2p": wp["b2p"], "r0": cc["r0"], "mask1": cc["mask1"],
                     "otab": cc["otab"], "idn_bf": cc["idn_bf"],
                     "idn_f32": cc["idn_f32"], "ones_bf": cc["ones_bf"],
                     "ones_f32r": cc["ones_f32r"]})
    resB = run_bass_kernel_spmd(_CACHE["B"], in_B, core_ids=list(range(8)))

    out = np.empty((NB, S, D), dtype=np.float32)
    for c in range(8):
        b, j = c // 4, c % 4
        oc = np.asarray(resB.results[c]["out"]).reshape(D, TOK)
        out[b][rows[j], :] = oc.T
    return out


# version 03
# speedup vs baseline: 1.0362x; 1.0024x over previous
import math
from contextlib import ExitStack

import numpy as np
import ml_dtypes

import concourse.bass as bass
import concourse.mybir as mybir
from concourse.bass_utils import run_bass_kernel_spmd

F32 = mybir.dt.float32
BF16 = mybir.dt.bfloat16
AF = mybir.ActivationFunctionType
ALU = mybir.AluOpType
AX = mybir.AxisListType.X
bfdt = ml_dtypes.bfloat16

NB, S, D = 2, 4096, 4096
NH, HD = 32, 128
INTER, RANK = 11008, 1024
TOK = 1024
NPT = 32
NI = 86
SCALE = 1.0 / math.sqrt(HD)
EPS = 1e-5
NEG = -1e30
SLOPES = [2.0 ** (-(i + 1) / 4.0) for i in range(NH)]


def _scidx(i, c):
    return i * (i + 1) // 2 + c


N_SC = 36



def _pack_weights(Wq, Wkv, Wo, U1, V1, U2, V2, b1, b2):
    p = {}
    p["u1p"] = np.ascontiguousarray(
        U1.reshape(32, 128, 8, 128).transpose(2, 1, 0, 3).reshape(8, 128, 4096))
    v1t = V1.reshape(8, 128, 172, 128).transpose(2, 1, 0, 3).reshape(172, 128, 1024)
    u2t = U2.reshape(86, 128, 8, 128).reshape(86, 128, 1024)
    p["v1u2p"] = np.ascontiguousarray(
        np.concatenate([v1t[:86], v1t[86:], u2t], axis=2))
    p["v2p"] = np.ascontiguousarray(
        V2.reshape(8, 128, 32, 128).transpose(2, 1, 0, 3).reshape(32, 128, 1024))
    p["wqp"] = np.ascontiguousarray(
        Wq.reshape(32, 128, 16, 256).transpose(2, 1, 0, 3).reshape(16, 128, 8192))
    p["wop"] = np.ascontiguousarray(
        Wo.reshape(32, 128, 16, 256).transpose(2, 1, 0, 3).reshape(16, 128, 8192))
    p["wkvp"] = np.ascontiguousarray(
        Wkv.reshape(32, 128, 256).transpose(1, 0, 2).reshape(128, 8192))
    p["b1p"] = np.ascontiguousarray(b1.reshape(172, 128).T)
    p["b2p"] = np.ascontiguousarray(b2.reshape(32, 128).T)
    return p


def _consts_for_core(j):
    c = {}
    dq = np.arange(128, dtype=np.float32)[:, None]
    dk = np.arange(512, dtype=np.float32)[None, :]
    c["r0"] = np.ascontiguousarray((dk - dq).astype(np.float32))
    c["mask1"] = np.where(dk - dq <= 128.0 * j, 0.0, NEG).astype(np.float32)
    ot = np.zeros((NH, N_SC), dtype=np.float32)
    for h in range(NH):
        for i in range(8):
            for cc in range(i + 1):
                ot[h, _scidx(i, cc)] = SLOPES[h] * (512.0 * cc - 512.0 * i - 128.0 * j)
    c["otab"] = np.ascontiguousarray(
        np.broadcast_to(ot.reshape(1, NH * N_SC), (128, NH * N_SC)).astype(np.float32))
    c["idn_bf"] = np.eye(128, dtype=bfdt)
    c["idn_f32"] = np.eye(128, dtype=np.float32)
    c["ones_bf"] = np.ones((128, 1), dtype=bfdt)
    c["ones_f32r"] = np.ones((1, 128), dtype=np.float32)
    return c


def _core_token_rows(j):
    idx = []
    for i in range(8):
        t0 = 128 * (4 * i + j)
        idx.extend(range(t0, t0 + 128))
    return np.array(idx, dtype=np.int64)




class Sems:
    def __init__(self, nc, es, names):
        self.h = {n: es.enter_context(nc.semaphore(n)) for n in names}
        self.n = {n: 0 for n in names}

    def inc(self, instr, name, k=1):
        instr.then_inc(self.h[name], k)
        self.n[name] += k
        return self.n[name]

    def __getitem__(self, name):
        return self.h[name]


def _common_bufs(nc, es, pfx=""):
    sb = lambda n, sh, dt: es.enter_context(nc.sbuf_tensor(pfx + n, sh, dt))
    bufs = {
        "xst": [sb(f"xst{i}", [128, 1024], F32) for i in range(2)],
        "xbf": [sb(f"xbf{i}", [128, 1024], BF16) for i in range(2)],
        "xsq": [sb(f"xsq{i}", [128, 1024], BF16) for i in range(2)],
        "row": sb("row", [1, 4096], F32),
        "mb_sb": sb("mb", [128, 1024], F32),
        "ib_sb": sb("ib", [128, 1024], F32),
        "u1st": [sb(f"u1st{i}", [128, 512], F32) for i in range(2)],
        "u1bf": [sb(f"u1bf{i}", [128, 512], BF16) for i in range(2)],
        "wst": [sb(f"wst{i}", [128, 1536], F32) for i in range(2)],
        "wbf": [sb(f"wbf{i}", [128, 3072], BF16) for i in range(2)],
        "v2st": [sb(f"v2st{i}", [128, 512], F32) for i in range(2)],
        "v2bf": [sb(f"v2bf{i}", [128, 1024], BF16) for i in range(2)],
        "sg_sb": sb("sg", [128, 1024], F32),
        "ub_sb": sb("ub", [128, 1024], F32),
        "act_bf": [sb(f"act{i}", [128, 1024], BF16) for i in range(2)],
        "b1p": sb("b1ps", [128, 172], F32),
        "b2h": sb("b2h", [128, 32], F32),
        "ones_bf": sb("onesbf", [128, 1], BF16),
        "ones_f32r": sb("onesf", [1, 128], F32),
    }
    bufs["xst2"] = bufs["xst"]
    bufs["t_sb"] = [bufs["sg_sb"], bufs["ub_sb"]]
    return bufs


LN_SEMS = ["lnld0", "lnld1", "lncast", "lnmm", "lnfin", "lnap",
           "lnld20", "lnld21"]
FFN_SEMS = ["fu1ld0", "fu1ld1", "fu1c", "fu1u", "fpev",
            "fwld0", "fwld1", "fwc",
            "fhmm", "fhdr", "fact", "fzmm", "fzd",
            "fv2ld0", "fv2ld1", "fv2c",
            "fomm", "ft", "fxld0", "fxld1", "fxadd", "fst0", "fst1", "fzcast"]


_LN_N = [0]


def _emit_ln(nc, sm, bufs, src_dram, y_sb, pre=None):
    _LN_N[0] += 1
    tag = f"_{_LN_N[0]}"
    with ExitStack() as pse:
        psf = lambda n, sh: pse.enter_context(nc.psum_tensor(n + tag, sh, F32))
        sx = [psf(f"lnsx{i}", [1, 512]) for i in range(2)]
        sq = [psf(f"lnsq{i}", [1, 512]) for i in range(2)]
        bc = [psf(f"lnbc{i}", [128, 512]) for i in range(4)]
        xst, xbf, xsq = bufs["xst"], bufs["xbf"], bufs["xsq"]
        row, mb_sb, ib_sb = bufs["row"], bufs["mb_sb"], bufs["ib_sb"]
        ones_bf, ones_f32r = bufs["ones_bf"], bufs["ones_f32r"]
        b_ld = [sm.n["lnld0"], sm.n["lnld1"]]
        b_ld2 = [sm.n["lnld20"], sm.n["lnld21"]]
        b_cast = sm.n["lncast"]; b_mm = sm.n["lnmm"]
        b_fin = sm.n["lnfin"]; b_ap = sm.n["lnap"]

        with nc.Block() as blk:
            @blk.gpsimd
            def _(g):
                for pt in range(NPT):
                    if pt == 0 and pre is not None:
                        for pn, pv in pre:
                            g.wait_ge(sm[pn], pv)
                    if pt >= 2:
                        g.wait_ge(sm["lncast"], b_cast + pt - 1)
                    sm.inc(g.dma_start(out=xst[pt % 2][:, :], in_=src_dram[pt, :, :]),
                           f"lnld{pt % 2}", 16)

            @blk.scalar
            def _(s):
                for pt in range(NPT):
                    s.wait_ge(sm[f"lnld{pt % 2}"], b_ld[pt % 2] + 16 * (pt // 2 + 1))
                    if pt >= 2:
                        s.wait_ge(sm["lnmm"], b_mm + pt - 1)
                    s.activation(xbf[pt % 2][:, :], xst[pt % 2][:, :], AF.Copy)
                    sm.inc(s.activation(xsq[pt % 2][:, :], xst[pt % 2][:, :],
                                        AF.Square), "lncast")
                s.wait_ge(sm["lnmm"], b_mm + NPT)
                for h2 in range(2):
                    last = s.activation(row[:, h2 * 512:(h2 + 1) * 512], sx[h2][:, :],
                                        AF.Copy, scale=1.0 / 4096.0)
                sm.inc(last, "lnfin")
                s.wait_ge(sm["lnfin"], b_fin + 4)
                sm.inc(s.activation(row[:, 3072:4096], row[:, 2048:3072], AF.Sqrt),
                       "lnfin")
                s.wait_ge(sm["lnmm"], b_mm + NPT + 1)
                for h2 in range(2):
                    s.activation(mb_sb[:, h2 * 512:(h2 + 1) * 512], bc[h2][:, :], AF.Copy)
                    last = s.activation(ib_sb[:, h2 * 512:(h2 + 1) * 512],
                                        bc[2 + h2][:, :], AF.Copy)
                sm.inc(last, "lnfin")
                for pt in range(NPT):
                    if pt < 2:
                        s.wait_ge(sm["lncast"], b_cast + 31 + pt)
                    else:
                        s.wait_ge(sm["lnap"], b_ap + 2 * (pt - 1))
                    sm.inc(s.dma_start(out=xst[pt % 2][:, :], in_=src_dram[pt, :, :]),
                           f"lnld2{pt % 2}", 16)

            @blk.tensor
            def _(t):
                for pt in range(NPT):
                    t.wait_ge(sm["lncast"], b_cast + pt + 1)
                    for h2 in range(2):
                        c0 = h2 * 512
                        t.matmul(sx[h2][:, :], ones_bf[:, :], xbf[pt % 2][:, c0:c0 + 512],
                                 start=(pt == 0), stop=(pt == NPT - 1))
                        last = t.matmul(sq[h2][:, :], ones_bf[:, :],
                                        xsq[pt % 2][:, c0:c0 + 512],
                                        start=(pt == 0), stop=(pt == NPT - 1))
                    sm.inc(last, "lnmm")
                t.wait_ge(sm["lnfin"], b_fin + 6)
                for h2 in range(2):
                    t.matmul(bc[h2][:, :], ones_f32r[:, :],
                             row[:, h2 * 512:(h2 + 1) * 512], start=True, stop=True)
                    last = t.matmul(bc[2 + h2][:, :], ones_f32r[:, :],
                                    row[:, 1024 + h2 * 512:1024 + (h2 + 1) * 512],
                                    start=True, stop=True)
                sm.inc(last, "lnmm")

            @blk.vector
            def _(v):
                v.wait_ge(sm["lnfin"], b_fin + 1)
                sm.inc(v.tensor_mul(row[:, 1024:2048], row[:, 0:1024], row[:, 0:1024]),
                       "lnfin")
                v.wait_ge(sm["lnfin"], b_fin + 2)
                for h2 in range(2):
                    last = v.scalar_tensor_tensor(
                        row[:, 2048 + h2 * 512:2048 + (h2 + 1) * 512], sq[h2][:, :],
                        1.0 / 4096.0, row[:, 1024 + h2 * 512:1024 + (h2 + 1) * 512],
                        ALU.mult, ALU.subtract)
                sm.inc(last, "lnfin")
                v.wait_ge(sm["lnfin"], b_fin + 3)
                sm.inc(v.tensor_scalar(row[:, 2048:3072], row[:, 2048:3072], EPS,
                                       None, ALU.add), "lnfin")
                v.wait_ge(sm["lnfin"], b_fin + 5)
                sm.inc(v.reciprocal(row[:, 1024:2048], row[:, 3072:4096]), "lnfin")

            @blk.vector
            def _(v):
                v.wait_ge(sm["lnfin"], b_fin + 7)
                for pt in range(NPT):
                    v.wait_ge(sm[f"lnld2{pt % 2}"], b_ld2[pt % 2] + 16 * (pt // 2 + 1))
                    sm.inc(v.tensor_sub(xst[pt % 2][:, :], xst[pt % 2][:, :],
                                        mb_sb[:, :]), "lnap")
                    v.wait_ge(sm["lnap"], b_ap + 2 * pt + 1)
                    sm.inc(v.tensor_mul(y_sb[:, pt * 1024:(pt + 1) * 1024],
                                        xst[pt % 2][:, :], ib_sb[:, :]), "lnap")


def _emit_ffn(nc, sm, bufs, y_sb, p_sb, zacc, xsrc, xdst, wd, is_final):
    with ExitStack() as pse:
        psf = lambda n, sh: pse.enter_context(nc.psum_tensor(n, sh, F32))
        q4 = [psf(f"fq{i}", [128, 512]) for i in range(4)]
        zp = [psf(f"fzp{i}", [128, 1024]) for i in range(2)]
        u1st, u1bf = bufs["u1st"], bufs["u1bf"]
        wst, wbf = bufs["wst"], bufs["wbf"]
        v2st, v2bf = bufs["v2st"], bufs["v2bf"]
        sg_sb, ub_sb = bufs["sg_sb"], bufs["ub_sb"]
        act_bf, t_sb, xst2 = bufs["act_bf"], bufs["t_sb"], bufs["xst2"]
        b1p, b2h = bufs["b1p"], bufs["b2h"]
        b = {k: sm.n[k] for k in FFN_SEMS}

        with nc.Block() as blk:
            @blk.sync
            def _(s):
                for pm in range(8):
                    for kc in range(8):
                        n = pm * 8 + kc
                        if n >= 2:
                            s.wait_ge(sm["fu1c"], b["fu1c"] + n - 1)
                        sm.inc(s.dma_start(out=u1st[n % 2][:, :],
                                           in_=wd["u1p"][pm, :, kc * 512:(kc + 1) * 512]),
                               f"fu1ld{n % 2}", 16)
                for i in range(NI):
                    for hf in range(2):
                        n = 2 * i + hf
                        if n >= 2:
                            s.wait_ge(sm["fwc"], b["fwc"] + n - 1)
                        sm.inc(s.dma_start(
                            out=wst[n % 2][:, :],
                            in_=wd["v1u2p"][i, :, hf * 1536:(hf + 1) * 1536]),
                               f"fwld{n % 2}", 16)
                for om in range(34):
                    if om < 32:
                        for hf in range(2):
                            n = 2 * om + hf
                            if n >= 2:
                                s.wait_ge(sm["fv2c"], b["fv2c"] + n - 1)
                            sm.inc(s.dma_start(
                                out=v2st[n % 2][:, :],
                                in_=wd["v2p"][om, :, hf * 512:(hf + 1) * 512]),
                                   f"fv2ld{n % 2}", 16)
                    ol = om - 1
                    if 0 <= ol < 32:
                        if ol >= 2:
                            s.wait_ge(sm[f"fst{ol % 2}"],
                                      b[f"fst{ol % 2}"] + 16 * (ol // 2))
                        sm.inc(s.dma_start(out=xst2[ol % 2][:, :],
                                           in_=xsrc[ol, :, :]),
                               f"fxld{ol % 2}", 16)

            @blk.gpsimd
            def _(g):
                for os_ in range(32):
                    g.wait_ge(sm["fxadd"], b["fxadd"] + os_ + 1)
                    sm.inc(g.dma_start(out=xdst[os_, :, :],
                                       in_=xst2[os_ % 2][:, :]),
                           f"fst{os_ % 2}", 16)
                if is_final:
                    g.wait_ge(sm["fst0"], b["fst0"] + 16 * 16)
                    g.wait_ge(sm["fst1"], b["fst1"] + 16 * 16)

            @blk.scalar
            def _(s):
                def u1cast(n):
                    s.wait_ge(sm[f"fu1ld{n % 2}"],
                              b[f"fu1ld{n % 2}"] + 16 * (n // 2 + 1))
                    if n >= 2:
                        s.wait_ge(sm["fu1u"], b["fu1u"] + n - 1)
                    sm.inc(s.activation(u1bf[n % 2][:, :], u1st[n % 2][:, :],
                                        AF.Copy), "fu1c")

                u1cast(0)
                u1cast(1)
                for pm in range(8):
                    for kc in range(8):
                        n = pm * 8 + kc
                        if n + 2 < 64:
                            u1cast(n + 2)
                    if pm >= 1:
                        s.wait_ge(sm["fu1u"], b["fu1u"] + 8 * pm)
                        for h2 in range(2):
                            c0 = (pm - 1) * 1024 + h2 * 512
                            last = s.activation(p_sb[:, c0:c0 + 512],
                                                q4[2 * ((pm - 1) % 2) + h2][:, :],
                                                AF.Copy)
                        sm.inc(last, "fpev")
                s.wait_ge(sm["fu1u"], b["fu1u"] + 64)
                for h2 in range(2):
                    c0 = 7 * 1024 + h2 * 512
                    last = s.activation(p_sb[:, c0:c0 + 512], q4[2 + h2][:, :], AF.Copy)
                sm.inc(last, "fpev")

                def wcast(n):
                    i2, hf = n // 2, n % 2
                    s.wait_ge(sm[f"fwld{n % 2}"],
                              b[f"fwld{n % 2}"] + 16 * (n // 2 + 1))
                    if i2 >= 2 and hf == 0:
                        s.wait_ge(sm["fzmm"], b["fzmm"] + 8 * (i2 - 1))
                    sm.inc(s.activation(wbf[i2 % 2][:, hf * 1536:(hf + 1) * 1536],
                                        wst[n % 2][:, :], AF.Copy), "fwc")

                for n in range(4):
                    wcast(n)
                for i in range(NI):
                    s.wait_ge(sm["fhmm"], b["fhmm"] + i + 1)
                    if i >= 1:
                        s.wait_ge(sm["fact"], b["fact"] + i)
                    for h2 in range(2):
                        c0 = h2 * 512
                        s.activation(sg_sb[:, c0:c0 + 512], q4[h2][:, :], AF.Silu,
                                     bias=b1p[:, i:i + 1])
                        last = s.activation(ub_sb[:, c0:c0 + 512], q4[2 + h2][:, :],
                                            AF.Identity, bias=b1p[:, 86 + i:87 + i])
                    sm.inc(last, "fhdr")
                    for hf in range(2):
                        n = 2 * (i + 2) + hf
                        if n < 2 * NI:
                            wcast(n)
                s.wait_ge(sm["fzd"], b["fzd"] + NI * 8)
                sm.inc(s.activation(p_sb[:, :], zacc[:, :], AF.Copy), "fzcast")

                def v2cast(n):
                    om2, hf = n // 2, n % 2
                    s.wait_ge(sm[f"fv2ld{n % 2}"],
                              b[f"fv2ld{n % 2}"] + 16 * (n // 2 + 1))
                    if om2 >= 2 and hf == 0:
                        s.wait_ge(sm["fomm"], b["fomm"] + om2 - 1)
                    sm.inc(s.activation(v2bf[om2 % 2][:, hf * 512:(hf + 1) * 512],
                                        v2st[n % 2][:, :], AF.Copy), "fv2c")

                for n in range(4):
                    v2cast(n)
                for om in range(32):
                    s.wait_ge(sm["fomm"], b["fomm"] + om + 1)
                    if om >= 2:
                        s.wait_ge(sm["fxadd"], b["fxadd"] + om - 1)
                    for h2 in range(2):
                        last = s.activation(t_sb[om % 2][:, h2 * 512:(h2 + 1) * 512],
                                            q4[2 * (om % 2) + h2][:, :], AF.Identity,
                                            scale=0.5, bias=b2h[:, om:om + 1])
                    sm.inc(last, "ft")
                    for hf in range(2):
                        n = 2 * (om + 2) + hf
                        if n < 64:
                            v2cast(n)

            @blk.tensor
            def _(t):
                for pm in range(8):
                    if pm >= 2:
                        t.wait_ge(sm["fpev"], b["fpev"] + pm - 1)
                    for kc in range(8):
                        n = pm * 8 + kc
                        t.wait_ge(sm["fu1c"], b["fu1c"] + n + 1)
                        for j in range(4):
                            k = kc * 4 + j
                            for h2 in range(2):
                                last = t.matmul(q4[2 * (pm % 2) + h2][:, :],
                                                u1bf[n % 2][:, j * 128:(j + 1) * 128],
                                                y_sb[:, k * 1024 + h2 * 512:k * 1024 + (h2 + 1) * 512],
                                                start=(k == 0), stop=(k == 31))
                        sm.inc(last, "fu1u")
                for i in range(NI + 1):
                    if i < NI:
                        t.wait_ge(sm["fwc"], b["fwc"] + 2 * (i + 1))
                        t.wait_ge(sm["fhdr"], b["fhdr"] + i)
                        if i == 0:
                            t.wait_ge(sm["fpev"], b["fpev"] + 8)
                        for part in range(2):
                            for k in range(8):
                                for h2 in range(2):
                                    cw = part * 1024 + k * 128
                                    last = t.matmul(
                                        q4[2 * part + h2][:, :],
                                        wbf[i % 2][:, cw:cw + 128],
                                        p_sb[:, k * 1024 + h2 * 512:k * 1024 + (h2 + 1) * 512],
                                        start=(k == 0), stop=(k == 7))
                        sm.inc(last, "fhmm")
                    if i >= 1:
                        ii = i - 1
                        t.wait_ge(sm["fact"], b["fact"] + ii + 1)
                        for zm in range(8):
                            g = ii * 8 + zm
                            if g >= 2:
                                t.wait_ge(sm["fzd"], b["fzd"] + g - 1)
                            for h2 in range(2):
                                last = t.matmul(
                                    zp[g % 2][:, h2 * 512:(h2 + 1) * 512],
                                    wbf[ii % 2][:, 2048 + zm * 128:2048 + (zm + 1) * 128],
                                    act_bf[ii % 2][:, h2 * 512:(h2 + 1) * 512],
                                    start=True, stop=True)
                            sm.inc(last, "fzmm")
                for om in range(32):
                    t.wait_ge(sm["fv2c"], b["fv2c"] + 2 * (om + 1))
                    if om == 0:
                        t.wait_ge(sm["fzcast"], b["fzcast"] + 1)
                        t.wait_ge(sm["fhdr"], b["fhdr"] + NI)
                    if om >= 2:
                        t.wait_ge(sm["ft"], b["ft"] + om - 1)
                    for k in range(8):
                        for h2 in range(2):
                            last = t.matmul(q4[2 * (om % 2) + h2][:, :],
                                            v2bf[om % 2][:, k * 128:(k + 1) * 128],
                                            p_sb[:, k * 1024 + h2 * 512:k * 1024 + (h2 + 1) * 512],
                                            start=(k == 0), stop=(k == 7))
                    sm.inc(last, "fomm")

            @blk.vector
            def _(v):
                for i in range(NI):
                    if i >= 1:
                        ii = i - 1
                        for zm in range(8):
                            g = ii * 8 + zm
                            v.wait_ge(sm["fzmm"], b["fzmm"] + g + 1)
                            dst = zacc[:, zm * 1024:(zm + 1) * 1024]
                            if ii == 0:
                                sm.inc(v.tensor_copy(dst, zp[g % 2][:, :]), "fzd")
                            else:
                                sm.inc(v.tensor_add(dst, dst, zp[g % 2][:, :]), "fzd")
                    v.wait_ge(sm["fhdr"], b["fhdr"] + i + 1)
                    if i >= 2:
                        v.wait_ge(sm["fzmm"], b["fzmm"] + 8 * (i - 1))
                    sm.inc(v.tensor_mul(act_bf[i % 2][:, :], sg_sb[:, :], ub_sb[:, :]),
                           "fact")
                ii = NI - 1
                for zm in range(8):
                    g = ii * 8 + zm
                    v.wait_ge(sm["fzmm"], b["fzmm"] + g + 1)
                    dst = zacc[:, zm * 1024:(zm + 1) * 1024]
                    sm.inc(v.tensor_add(dst, dst, zp[g % 2][:, :]), "fzd")
                for om in range(32):
                    v.wait_ge(sm["ft"], b["ft"] + om + 1)
                    v.wait_ge(sm[f"fxld{om % 2}"],
                              b[f"fxld{om % 2}"] + 16 * (om // 2 + 1))
                    sm.inc(v.tensor_add(xst2[om % 2][:, :], xst2[om % 2][:, :],
                                        t_sb[om % 2][:, :]), "fxadd")




def build_A():
    nc = bass.Bass("TRN2", target_bir_lowering=False)
    X = nc.dram_tensor("x", [NPT, 128, TOK], F32, kind="ExternalInput")
    U1P = nc.dram_tensor("u1p", [8, 128, 4096], F32, kind="ExternalInput")
    V1U2P = nc.dram_tensor("v1u2p", [86, 128, 3072], F32, kind="ExternalInput")
    V2P = nc.dram_tensor("v2p", [32, 128, 1024], F32, kind="ExternalInput")
    WKVP = nc.dram_tensor("wkvp", [128, 8192], F32, kind="ExternalInput")
    B1P = nc.dram_tensor("b1p", [128, 172], F32, kind="ExternalInput")
    B2P = nc.dram_tensor("b2p", [128, 32], F32, kind="ExternalInput")
    ONESBF = nc.dram_tensor("ones_bf", [128, 1], BF16, kind="ExternalInput")
    ONESF = nc.dram_tensor("ones_f32r", [1, 128], F32, kind="ExternalInput")
    X1 = nc.dram_tensor("x1", [NPT, 128, TOK], F32, kind="ExternalOutput")
    Y2 = nc.dram_tensor("y2", [NPT, 128, TOK], BF16, kind="ExternalOutput")
    KV = nc.dram_tensor("kv", [2, 128, TOK], BF16, kind="ExternalOutput")
    wd = {"u1p": U1P, "v1u2p": V1U2P, "v2p": V2P}

    with ExitStack() as es:
        sb = lambda n, sh, dt: es.enter_context(nc.sbuf_tensor(n, sh, dt))
        y_sb = sb("y", [128, NPT * 1024], BF16)
        p_sb = sb("p", [128, 8 * 1024], BF16)
        zacc = sb("zacc", [128, 8 * 1024], F32)
        bufs = _common_bufs(nc, es)
        kvw = bufs["u1st"]
        kvwb = bufs["u1bf"]
        kv_bf = bufs["act_bf"]
        sems = Sems(nc, es, ["init"] + LN_SEMS + FFN_SEMS +
                    ["kvld0", "kvld1", "kvc", "kvmm", "kvev", "yst"])

        with nc.Block() as blk:
            @blk.gpsimd
            def _(g):
                for dst, src in ((bufs["b1p"], B1P), (bufs["b2h"], B2P),
                                 (bufs["ones_bf"], ONESBF), (bufs["ones_f32r"], ONESF)):
                    sems.inc(g.dma_start(out=dst[:, :], in_=src[:, :]), "init", 16)

            @blk.vector
            def _(v):
                v.wait_ge(sems["init"], 64)
                v.tensor_scalar(bufs["b2h"][:, :], bufs["b2h"][:, :], 0.5, None,
                                ALU.mult)

        _emit_ln(nc, sems, bufs, X, y_sb)
        _emit_ffn(nc, sems, bufs, y_sb, p_sb, zacc, X, X1, wd, is_final=False)
        _emit_ln(nc, sems, bufs, X1, y_sb,
                 pre=[("fst0", sems.n["fst0"]), ("fst1", sems.n["fst1"])])

        with ExitStack() as pse:
            kvp = [pse.enter_context(nc.psum_tensor(f"kvp{i}", [128, 512], F32))
                   for i in range(4)]
            with nc.Block() as blk:
                @blk.gpsimd
                def _(g):
                    for pt in range(NPT):
                        sems.inc(g.dma_start(out=Y2[pt, :, :],
                                             in_=y_sb[:, pt * 1024:(pt + 1) * 1024]),
                                 "yst", 16)

                @blk.sync
                def _(s):
                    for k in range(32):
                        if k >= 2:
                            s.wait_ge(sems["kvc"], k - 1)
                        sems.inc(s.dma_start(out=kvw[k % 2][:, 0:256],
                                             in_=WKVP[:, k * 256:(k + 1) * 256]),
                                 f"kvld{k % 2}", 16)

                @blk.scalar
                def _(s):
                    for k in range(32):
                        s.wait_ge(sems[f"kvld{k % 2}"], 16 * (k // 2 + 1))
                        if k >= 2:
                            s.wait_ge(sems["kvmm"], k - 1)
                        sems.inc(s.activation(kvwb[k % 2][:, 0:256],
                                              kvw[k % 2][:, 0:256], AF.Copy), "kvc")
                    s.wait_ge(sems["kvmm"], 32)
                    for m2 in range(2):
                        for h2 in range(2):
                            last = s.activation(kv_bf[m2][:, h2 * 512:(h2 + 1) * 512],
                                                kvp[2 * m2 + h2][:, :], AF.Copy)
                    sems.inc(last, "kvev")

                @blk.tensor
                def _(t):
                    for k in range(32):
                        t.wait_ge(sems["kvc"], k + 1)
                        for m2 in range(2):
                            for h2 in range(2):
                                last = t.matmul(
                                    kvp[2 * m2 + h2][:, :],
                                    kvwb[k % 2][:, m2 * 128:(m2 + 1) * 128],
                                    y_sb[:, k * 1024 + h2 * 512:k * 1024 + (h2 + 1) * 512],
                                    start=(k == 0), stop=(k == 31))
                        sems.inc(last, "kvmm")

                @blk.gpsimd
                def _(g):
                    g.wait_ge(sems["kvev"], 1)
                    for m2 in range(2):
                        sems.inc(g.dma_start(out=KV[m2, :, :], in_=kv_bf[m2][:, :]),
                                 "yst", 16)
                    g.wait_ge(sems["yst"], 16 * (NPT + 2))
                    g.wait_ge(sems["fst0"], 16 * 16)
                    g.wait_ge(sems["fst1"], 16 * 16)
    return nc




def build_B():
    nc = bass.Bass("TRN2", target_bir_lowering=False)
    X1 = nc.dram_tensor("x1", [NPT, 128, TOK], F32, kind="ExternalInput")
    Y2D = nc.dram_tensor("y2", [NPT, 128, TOK], BF16, kind="ExternalInput")
    KVG = nc.dram_tensor("kvg", [4, 2, 128, TOK], BF16, kind="ExternalInput")
    WQP = nc.dram_tensor("wqp", [16, 128, 8192], F32, kind="ExternalInput")
    WOP = nc.dram_tensor("wop", [16, 128, 8192], F32, kind="ExternalInput")
    U1P = nc.dram_tensor("u1p", [8, 128, 4096], F32, kind="ExternalInput")
    V1U2P = nc.dram_tensor("v1u2p", [86, 128, 3072], F32, kind="ExternalInput")
    V2P = nc.dram_tensor("v2p", [32, 128, 1024], F32, kind="ExternalInput")
    B1P = nc.dram_tensor("b1p", [128, 172], F32, kind="ExternalInput")
    B2P = nc.dram_tensor("b2p", [128, 32], F32, kind="ExternalInput")
    R0D = nc.dram_tensor("r0", [128, 512], F32, kind="ExternalInput")
    MK1 = nc.dram_tensor("mask1", [128, 512], F32, kind="ExternalInput")
    OTAB = nc.dram_tensor("otab", [128, NH * N_SC], F32, kind="ExternalInput")
    IDNB = nc.dram_tensor("idn_bf", [128, 128], BF16, kind="ExternalInput")
    IDNF = nc.dram_tensor("idn_f32", [128, 128], F32, kind="ExternalInput")
    ONESBF = nc.dram_tensor("ones_bf", [128, 1], BF16, kind="ExternalInput")
    ONESF = nc.dram_tensor("ones_f32r", [1, 128], F32, kind="ExternalInput")
    OUT = nc.dram_tensor("out", [NPT, 128, TOK], F32, kind="ExternalOutput")
    XB = nc.dram_tensor("xbuf", [NPT, 128, TOK], F32, kind="Internal")
    wd = {"u1p": U1P, "v1u2p": V1U2P, "v2p": V2P}

    with ExitStack() as es0:
        sb0 = lambda n, sh, dt: es0.enter_context(nc.sbuf_tensor(n, sh, dt))
        b1p_sb = sb0("b1ps", [128, 172], F32)
        b2h_sb = sb0("b2h", [128, 32], F32)
        onesbf_sb = sb0("onesbf", [128, 1], BF16)
        onesf_sb = sb0("onesf", [1, 128], F32)
        sems = Sems(nc, es0, ["init"] + LN_SEMS + FFN_SEMS + [
            "vktr", "vkev", "knsq", "knmm", "knev",
            "qwld0", "qwld1", "qwc", "qwu", "qev",
            "qnsq", "qnmm", "qnev", "bch",
            "scmm", "scv", "sexp", "strc", "ptev", "spv",
            "nsum", "nrow1", "nrec", "nrb2", "nrbc", "ndone", "relw",
            "wwld0", "wwld1", "wwc", "wwu",
            "wxld0", "wxld1", "wxadd", "wst20", "wst21",
        ])

        with ExitStack() as esA:
            sbA = lambda n, sh, dt: esA.enter_context(nc.sbuf_tensor(n, sh, dt))
            r0_sb = sbA("r0s", [128, 512], F32)
            mk_sb = sbA("mk1s", [128, 512], F32)
            ot_sb = sbA("ots", [128, NH * N_SC], F32)
            ba_sb = sbA("bas", [128, NH * N_SC], F32)
            idnb_sb = sbA("idnb", [128, 128], BF16)
            idnf_sb = sbA("idnf", [128, 128], F32)
            relh_sb = sbA("relh", [128, 512], F32)
            rm_sb = sbA("relm", [128, 512], F32)
            sc2_sb = [sbA(f"sc2{i}", [128, 512], F32) for i in range(2)]
            pr_sb = [sbA(f"pr{i}", [128, 512], BF16) for i in range(4)]
            pt_sb = sbA("pts", [128, 32 * 128], BF16)
            sums_sb = sbA("sumss", [128, 2 * N_SC], F32)
            ssum_sb = sbA("ssum", [128, 1], F32)
            srow_sb = sbA("srow", [1, 128], F32)
            rb_sb = sbA("rbs", [128, 128], F32)
            qsq_sb = sbA("qsqs", [128, 1024], BF16)
            nrow_sb = sbA("nrow", [1, 1024], F32)
            bsc_sb = sbA("bsc", [1, 96], F32)
            bcol_sb = sbA("bcol", [128, 32], F32)
            wqs = [sbA(f"wqs{i}", [128, 256], F32) for i in range(2)]
            wqb = [sbA(f"wqb{i}", [128, 256], BF16) for i in range(2)]
            kt_sb = sbA("kt", [128, 4096], BF16)
            vk_sb = sbA("vk", [128, 4096], BF16)
            y_sb = sbA("y", [128, NPT * 1024], BF16)
            q_sb = sbA("q", [128, NPT * 1024], BF16)

            with ExitStack() as esV:
                vt_sb = esV.enter_context(nc.sbuf_tensor("vt", [128, 4096], BF16))
                tp = [esV.enter_context(nc.psum_tensor(f"vtp{i}", [128, 128], BF16))
                      for i in range(2)]
                knp = esV.enter_context(nc.psum_tensor("knp", [1, 512], F32))

                with nc.Block() as blk:
                    @blk.gpsimd
                    def _(g):
                        for dst, src in ((b1p_sb, B1P), (b2h_sb, B2P),
                                         (onesbf_sb, ONESBF), (onesf_sb, ONESF),
                                         (r0_sb, R0D), (mk_sb, MK1), (ot_sb, OTAB),
                                         (idnb_sb, IDNB), (idnf_sb, IDNF)):
                            sems.inc(g.dma_start(out=dst[:, :], in_=src[:, :]),
                                     "init", 16)
                        for r in range(4):
                            sems.inc(g.dma_start(out=kt_sb[:, r * 1024:(r + 1) * 1024],
                                                 in_=KVG[r, 0, :, :]), "init", 16)
                            sems.inc(g.dma_start(out=vt_sb[:, r * 1024:(r + 1) * 1024],
                                                 in_=KVG[r, 1, :, :]), "init", 16)
                        for pt in range(NPT):
                            sems.inc(g.dma_start(out=y_sb[:, pt * 1024:(pt + 1) * 1024],
                                                 in_=Y2D[pt, :, :]), "init", 16)

                    @blk.vector
                    def _(v):
                        v.wait_ge(sems["init"], sems.n["init"])
                        v.tensor_scalar(b2h_sb[:, :], b2h_sb[:, :], 0.5, None,
                                        ALU.mult)

                order = [(r, i) for r in range(4) for i in range(8)]
                with nc.Block() as blk:
                    @blk.tensor
                    def _(t):
                        for n, (r, i) in enumerate(order):
                            if n >= 2:
                                t.wait_ge(sems["vkev"], n - 1)
                            sems.inc(t.transpose(
                                tp[n % 2][:, :],
                                vt_sb[:, r * 1024 + i * 128:r * 1024 + (i + 1) * 128],
                                idnb_sb[:, :]), "vktr")
                        for ch in range(4):
                            t.wait_ge(sems["knsq"], ch + 1)
                            for h2 in range(2):
                                n = 2 * ch + h2
                                if h2 == 1:
                                    t.wait_ge(sems["knev"], 3 * ch + 1)
                                elif ch >= 1:
                                    t.wait_ge(sems["knev"], 3 * ch - 1)
                                sems.inc(t.matmul(knp[:, :], onesbf_sb[:, :],
                                                  qsq_sb[:, h2 * 512:(h2 + 1) * 512],
                                                  start=True, stop=True), "knmm")

                    @blk.vector
                    def _(v):
                        for n, (r, i) in enumerate(order):
                            kt = 4 * i + r
                            v.wait_ge(sems["vktr"], n + 1)
                            sems.inc(v.tensor_copy(vk_sb[:, kt * 128:(kt + 1) * 128],
                                                   tp[n % 2][:, :]), "vkev")
                        for ch in range(4):
                            for h2 in range(2):
                                n = 2 * ch + h2
                                v.wait_ge(sems["knmm"], n + 1)
                                if h2 == 0 and ch >= 1:
                                    v.wait_ge(sems["knev"], 3 * ch)
                                sems.inc(v.tensor_copy(
                                    nrow_sb[:, h2 * 512:(h2 + 1) * 512], knp[:, :]),
                                    "knev")
                            v.wait_ge(sems["knev"], 3 * ch + 2)
                            sems.inc(v.reduce_max(bsc_sb[:, 64 + ch:65 + ch],
                                                  nrow_sb[:, 0:1024], axis=AX), "knev")
                        v.wait_ge(sems["knev"], 12)
                        sems.inc(v.reduce_max(bsc_sb[:, 0:1], bsc_sb[:, 64:68],
                                              axis=AX), "knev")

                    @blk.scalar
                    def _(s):
                        for ch in range(4):
                            if ch >= 1:
                                s.wait_ge(sems["knmm"], 2 * ch)
                            sems.inc(s.activation(qsq_sb[:, :],
                                                  kt_sb[:, ch * 1024:(ch + 1) * 1024],
                                                  AF.Square), "knsq")

            with ExitStack() as esQ:
                qp = [esQ.enter_context(nc.psum_tensor(f"qp{i}", [128, 512], F32))
                      for i in range(8)]
                with nc.Block() as blk:
                    @blk.sync
                    def _(s):
                        for g in range(16):
                            for k in range(32):
                                n = g * 32 + k
                                if n >= 2:
                                    s.wait_ge(sems["qwc"], n - 1)
                                sems.inc(s.dma_start(
                                    out=wqs[n % 2][:, :],
                                    in_=WQP[g, :, k * 256:(k + 1) * 256]),
                                    f"qwld{n % 2}", 16)

                    @blk.scalar
                    def _(s):
                        for g in range(16):
                            for k in range(32):
                                n = g * 32 + k
                                s.wait_ge(sems[f"qwld{n % 2}"], 16 * (n // 2 + 1))
                                if n >= 2:
                                    s.wait_ge(sems["qwu"], n - 1)
                                sems.inc(s.activation(wqb[n % 2][:, :], wqs[n % 2][:, :],
                                                      AF.Copy), "qwc")
                            s.wait_ge(sems["qwu"], 32 * (g + 1))
                            for m2 in range(2):
                                for h2 in range(2):
                                    hh = 2 * g + m2
                                    c0 = hh * 1024 + h2 * 512
                                    last = s.activation(
                                        q_sb[:, c0:c0 + 512],
                                        qp[4 * (g % 2) + 2 * m2 + h2][:, :], AF.Copy,
                                        scale=SCALE)
                            sems.inc(last, "qev")

                    @blk.tensor
                    def _(t):
                        for g in range(16):
                            if g >= 2:
                                t.wait_ge(sems["qev"], g - 1)
                            for k in range(32):
                                n = g * 32 + k
                                t.wait_ge(sems["qwc"], n + 1)
                                for m2 in range(2):
                                    for h2 in range(2):
                                        last = t.matmul(
                                            qp[4 * (g % 2) + 2 * m2 + h2][:, :],
                                            wqb[n % 2][:, m2 * 128:(m2 + 1) * 128],
                                            y_sb[:, k * 1024 + h2 * 512:k * 1024 + (h2 + 1) * 512],
                                            start=(k == 0), stop=(k == 31))
                                sems.inc(last, "qwu")

            with ExitStack() as esN:
                np1 = esN.enter_context(nc.psum_tensor("np1", [1, 512], F32))
                bbp = esN.enter_context(nc.psum_tensor("bbp", [128, 32], F32))
                with nc.Block() as blk:
                    @blk.scalar
                    def _(s):
                        for h in range(NH):
                            if h >= 1:
                                s.wait_ge(sems["qnmm"], 2 * h)
                            sems.inc(s.activation(qsq_sb[:, :],
                                                  q_sb[:, h * 1024:(h + 1) * 1024],
                                                  AF.Square), "qnsq")
                        s.wait_ge(sems["bch"], 1)
                        sems.inc(s.activation(bsc_sb[:, 32:64], bsc_sb[:, 32:64],
                                              AF.Sqrt), "bch")

                    @blk.tensor
                    def _(t):
                        for h in range(NH):
                            t.wait_ge(sems["qnsq"], h + 1)
                            for h2 in range(2):
                                if h2 == 1:
                                    t.wait_ge(sems["qnev"], 3 * h + 1)
                                elif h >= 1:
                                    t.wait_ge(sems["qnev"], 3 * h - 1)
                                sems.inc(t.matmul(np1[:, :], onesbf_sb[:, :],
                                                  qsq_sb[:, h2 * 512:(h2 + 1) * 512],
                                                  start=True, stop=True), "qnmm")
                        t.wait_ge(sems["bch"], 2)
                        sems.inc(t.matmul(bbp[:, :], onesf_sb[:, :], bsc_sb[:, 32:64],
                                          start=True, stop=True), "qnmm")

                    @blk.vector
                    def _(v):
                        for h in range(NH):
                            for h2 in range(2):
                                n = 2 * h + h2
                                v.wait_ge(sems["qnmm"], n + 1)
                                if h2 == 0 and h >= 1:
                                    v.wait_ge(sems["qnev"], 3 * h)
                                sems.inc(v.tensor_copy(
                                    nrow_sb[:, h2 * 512:(h2 + 1) * 512], np1[:, :]),
                                    "qnev")
                            v.wait_ge(sems["qnev"], 3 * h + 2)
                            sems.inc(v.reduce_max(bsc_sb[:, 1 + h:2 + h],
                                                  nrow_sb[:, 0:1024], axis=AX), "qnev")
                        v.wait_ge(sems["qnev"], 3 * NH)
                        sems.inc(v.tensor_scalar(bsc_sb[:, 32:64], bsc_sb[:, 1:33],
                                                 bsc_sb[:, 0:1], None, ALU.mult),
                                 "bch")

                    @blk.scalar
                    def _(s):
                        s.wait_ge(sems["qnmm"], 2 * NH + 1)
                        sems.inc(s.activation(bcol_sb[:, :], bbp[:, :], AF.Copy),
                                 "bch")

                    @blk.vector
                    def _(v):
                        v.wait_ge(sems["bch"], 3)
                        for h in range(NH):
                            sems.inc(v.tensor_scalar(ba_sb[:, h * N_SC:(h + 1) * N_SC],
                                                     ot_sb[:, h * N_SC:(h + 1) * N_SC],
                                                     bcol_sb[:, h:h + 1], None,
                                                     ALU.subtract), "bch")

            chunks = []
            slot_of = {}
            g = 0
            for h in range(NH):
                for i in range(8):
                    for c in range(i + 1):
                        chunks.append((h, i, c, g))
                        if c == i:
                            slot_of[g] = (h, i, h * 8 + i)
                        g += 1
            NG = g

            with ExitStack() as esE:
                scp = [esE.enter_context(nc.psum_tensor(f"scp{i}", [128, 512], F32))
                       for i in range(2)]
                ptp = [esE.enter_context(nc.psum_tensor(f"ptp{i}", [128, 512], BF16))
                       for i in range(2)]
                pvp = [esE.enter_context(nc.psum_tensor(f"pvp{i}", [128, 128], F32))
                       for i in range(2)]
                nrp = esE.enter_context(nc.psum_tensor("nrp", [1, 128], F32))
                rbp = esE.enter_context(nc.psum_tensor("rbp", [128, 128], F32))
                with nc.Block() as blk:
                    b_scmm = sems.n["scmm"]; b_scv = sems.n["scv"]
                    b_sexp = sems.n["sexp"]; b_strc = sems.n["strc"]
                    b_ptev = sems.n["ptev"]; b_spv = sems.n["spv"]
                    b_nsum = sems.n["nsum"]; b_nrow1 = sems.n["nrow1"]
                    b_nrec = sems.n["nrec"]; b_nrb2 = sems.n["nrb2"]
                    b_nrbc = sems.n["nrbc"]; b_ndone = sems.n["ndone"]

                    def kt_rhs(c):
                        return bass.AP(kt_sb, c * 128,
                                       [[4096, 128], [1024, 4], [1, 128]])

                    @blk.tensor
                    def _(t):
                        for gg in range(NG + 6):
                            if gg < NG:
                                h, i, c, _ = chunks[gg]
                                if gg >= 2:
                                    t.wait_ge(sems["scv"], b_scv + gg - 1)
                                sems.inc(t.matmul(
                                    scp[gg % 2][:, :],
                                    q_sb[:, h * 1024 + i * 128:h * 1024 + (i + 1) * 128],
                                    kt_rhs(c), start=True, stop=True), "scmm")
                            gp = gg - 2
                            if 0 <= gp < NG:
                                t.wait_ge(sems["sexp"], b_sexp + gp + 1)
                                if gp >= 2:
                                    t.wait_ge(sems["ptev"], b_ptev + gp - 1)
                                for bb in range(4):
                                    last = t.transpose(
                                        ptp[gp % 2][:, bb * 128:(bb + 1) * 128],
                                        pr_sb[gp % 4][:, bb * 128:(bb + 1) * 128],
                                        idnb_sb[:, :])
                                sems.inc(last, "strc")
                                if gp in slot_of:
                                    h2_, i2_, sl = slot_of[gp]
                                    t.wait_ge(sems["ptev"], b_ptev + gp + 1)
                                    if sl >= 2:
                                        t.wait_ge(sems["ndone"], b_ndone + sl - 1)
                                    nkt = 4 * (i2_ + 1)
                                    for kt in range(nkt):
                                        last = t.matmul(
                                            pvp[sl % 2][:, :],
                                            vk_sb[:, kt * 128:(kt + 1) * 128],
                                            pt_sb[:, kt * 128:(kt + 1) * 128],
                                            start=(kt == 0), stop=(kt == nkt - 1))
                                    sems.inc(last, "spv")
                                    t.wait_ge(sems["nsum"], b_nsum + sl + 1)
                                    if sl >= 1:
                                        t.wait_ge(sems["nrec"], b_nrec + sl)
                                    sems.inc(t.matmul(nrp[:, :], ssum_sb[:, :],
                                                      idnf_sb[:, :], start=True,
                                                      stop=True), "nrow1")
                            gm = gg - 3
                            if 0 <= gm < NG and gm in slot_of:
                                _, _, sl = slot_of[gm]
                                t.wait_ge(sems["nrec"], b_nrec + sl + 1)
                                if sl >= 1:
                                    t.wait_ge(sems["nrbc"], b_nrbc + sl)
                                sems.inc(t.matmul(rbp[:, :], onesf_sb[:, :],
                                                  srow_sb[:, :], start=True,
                                                  stop=True), "nrb2")

                    @blk.vector
                    def _(v):
                        for gg in range(NG + 6):
                            if gg < NG:
                                h, i, c, _ = chunks[gg]
                                if i == 0 and c == 0:
                                    if h >= 1:
                                        v.wait_ge(sems["scv"], b_scv + N_SC * h)
                                    v.scalar_tensor_tensor(rm_sb[:, :], r0_sb[:, :],
                                                           float(SLOPES[h]),
                                                           mk_sb[:, :], ALU.mult,
                                                           ALU.add)
                                    sems.inc(v.tensor_scalar(relh_sb[:, :], r0_sb[:, :],
                                                             float(SLOPES[h]), None,
                                                             ALU.mult), "relw")
                                    v.wait_ge(sems["relw"], h + 1)
                                v.wait_ge(sems["scmm"], b_scmm + gg + 1)
                                if gg >= 2:
                                    v.wait_ge(sems["sexp"], b_sexp + gg - 1)
                                rel = rm_sb if c == i else relh_sb
                                sems.inc(v.scalar_tensor_tensor(
                                    sc2_sb[gg % 2][:, :], scp[gg % 2][:, :], 1.0,
                                    rel[:, :], ALU.mult, ALU.add), "scv")
                            gp = gg - 2
                            if 0 <= gp < NG:
                                h2_, i2_, c2_, _ = chunks[gp]
                                v.wait_ge(sems["strc"], b_strc + gp + 1)
                                if c2_ == 0 and (h2_ * 8 + i2_) >= 1:
                                    v.wait_ge(sems["spv"], b_spv + h2_ * 8 + i2_)
                                sems.inc(v.tensor_copy(
                                    pt_sb[:, c2_ * 512:(c2_ + 1) * 512],
                                    ptp[gp % 2][:, :]), "ptev")
                                if gp in slot_of:
                                    _, _, sl = slot_of[gp]
                                    v.wait_ge(sems["sexp"], b_sexp + gp + 1)
                                    if sl >= 1:
                                        v.wait_ge(sems["nrow1"], b_nrow1 + sl)
                                    i0 = (h2_ % 2) * N_SC + _scidx(i2_, 0)
                                    sems.inc(v.reduce_sum(ssum_sb[:, :],
                                                          sums_sb[:, i0:i0 + i2_ + 1],
                                                          axis=AX), "nsum")
                            gr = gg - 3
                            if 0 <= gr < NG and gr in slot_of:
                                _, _, sl = slot_of[gr]
                                v.wait_ge(sems["nrow1"], b_nrow1 + sl + 1)
                                sems.inc(v.reciprocal(srow_sb[:, :], nrp[:, :]),
                                         "nrec")
                            gf = gg - 5
                            if 0 <= gf < NG and gf in slot_of:
                                h2_, i2_, sl = slot_of[gf]
                                v.wait_ge(sems["nrbc"], b_nrbc + sl + 1)
                                v.wait_ge(sems["spv"], b_spv + sl + 1)
                                c0 = h2_ * 1024 + i2_ * 128
                                sems.inc(v.tensor_mul(q_sb[:, c0:c0 + 128],
                                                      pvp[sl % 2][:, :], rb_sb[:, :]),
                                         "ndone")

                    @blk.scalar
                    def _(s):
                        for gg in range(NG + 6):
                            if gg < NG:
                                h, i, c, _ = chunks[gg]
                                if i == 0 and c == 0 and h >= 2:
                                    s.wait_ge(sems["nsum"], b_nsum + 8 * (h - 1))
                                s.wait_ge(sems["scv"], b_scv + gg + 1)
                                if gg >= 4:
                                    s.wait_ge(sems["strc"], b_strc + gg - 3)
                                col = h * N_SC + _scidx(i, c)
                                scol = (h % 2) * N_SC + _scidx(i, c)
                                sems.inc(s.activation(
                                    pr_sb[gg % 4][:, :], sc2_sb[gg % 2][:, :], AF.Exp,
                                    bias=ba_sb[:, col:col + 1],
                                    accum_out=sums_sb[:, scol:scol + 1]), "sexp")
                            gb = gg - 4
                            if 0 <= gb < NG and gb in slot_of:
                                _, _, sl = slot_of[gb]
                                s.wait_ge(sems["nrb2"], b_nrb2 + sl + 1)
                                if sl >= 1:
                                    s.wait_ge(sems["ndone"], b_ndone + sl)
                                sems.inc(s.activation(rb_sb[:, :], rbp[:, :], AF.Copy),
                                         "nrbc")

            with ExitStack() as esW:
                xsw = [esW.enter_context(nc.sbuf_tensor(f"xsw{i}", [128, 1024], F32))
                       for i in range(2)]
                op = [esW.enter_context(nc.psum_tensor(f"wop{i}", [128, 512], F32))
                      for i in range(8)]
                with nc.Block() as blk:
                    @blk.sync
                    def _(s):
                        for g in range(16):
                            for k in range(32):
                                n = g * 32 + k
                                if n >= 2:
                                    s.wait_ge(sems["wwc"], n - 1)
                                sems.inc(s.dma_start(
                                    out=wqs[n % 2][:, :],
                                    in_=WOP[g, :, k * 256:(k + 1) * 256]),
                                    f"wwld{n % 2}", 16)

                    @blk.scalar
                    def _(s):
                        for g in range(16):
                            for k in range(32):
                                n = g * 32 + k
                                s.wait_ge(sems[f"wwld{n % 2}"], 16 * (n // 2 + 1))
                                if n >= 2:
                                    s.wait_ge(sems["wwu"], n - 1)
                                sems.inc(s.activation(wqb[n % 2][:, :],
                                                      wqs[n % 2][:, :], AF.Copy),
                                         "wwc")

                    @blk.tensor
                    def _(t):
                        for g in range(16):
                            if g >= 1:
                                t.wait_ge(sems["wxadd"], 2 * (g - 1))
                            for k in range(32):
                                n = g * 32 + k
                                t.wait_ge(sems["wwc"], n + 1)
                                for m2 in range(2):
                                    for h2 in range(2):
                                        last = t.matmul(
                                            op[4 * (g % 2) + 2 * m2 + h2][:, :],
                                            wqb[n % 2][:, m2 * 128:(m2 + 1) * 128],
                                            q_sb[:, k * 1024 + h2 * 512:k * 1024 + (h2 + 1) * 512],
                                            start=(k == 0), stop=(k == 31))
                                sems.inc(last, "wwu")

                    @blk.gpsimd
                    def _(g):
                        for om in range(32):
                            if om >= 2:
                                g.wait_ge(sems[f"wst2{om % 2}"], 16 * (om // 2))
                            sems.inc(g.dma_start(out=xsw[om % 2][:, :],
                                                 in_=X1[om, :, :]),
                                     f"wxld{om % 2}", 16)
                            g.wait_ge(sems["wxadd"], om + 1)
                            sems.inc(g.dma_start(out=XB[om, :, :],
                                                 in_=xsw[om % 2][:, :]),
                                     f"wst2{om % 2}", 16)

                    @blk.vector
                    def _(v):
                        for om in range(32):
                            g2, m2 = om // 2, om % 2
                            v.wait_ge(sems["wwu"], 32 * (g2 + 1))
                            v.wait_ge(sems[f"wxld{om % 2}"], 16 * (om // 2 + 1))
                            for h2 in range(2):
                                last = v.tensor_add(
                                    xsw[om % 2][:, h2 * 512:(h2 + 1) * 512],
                                    xsw[om % 2][:, h2 * 512:(h2 + 1) * 512],
                                    op[4 * (g2 % 2) + 2 * m2 + h2][:, :])
                            sems.inc(last, "wxadd")

        with ExitStack() as esF:
            sbF = lambda n, sh, dt: esF.enter_context(nc.sbuf_tensor(n, sh, dt))
            y2_sb = sbF("y3", [128, NPT * 1024], BF16)
            p_sb = sbF("p", [128, 8 * 1024], BF16)
            zacc = sbF("zacc", [128, 8 * 1024], F32)
            bufs = _common_bufs(nc, esF, pfx="f_")
            bufs["b1p"] = b1p_sb
            bufs["b2h"] = b2h_sb
            bufs["ones_bf"] = onesbf_sb
            bufs["ones_f32r"] = onesf_sb
            _emit_ln(nc, sems, bufs, XB, y2_sb,
                     pre=[("wst20", sems.n["wst20"]), ("wst21", sems.n["wst21"])])
            _emit_ffn(nc, sems, bufs, y2_sb, p_sb, zacc, XB, OUT, wd, is_final=True)
    return nc



_CACHE = {}


def kernel(L0_attn_hidden_states, L0_attn_Wq, L0_attn_Wkv, L0_attn_Wo,
           L0_ffn_U1, L0_ffn_V1, L0_ffn_U2, L0_ffn_V2, L0_ffn_b1, L0_ffn_b2):
    x = np.asarray(L0_attn_hidden_states, dtype=np.float32)
    wp = _pack_weights(np.asarray(L0_attn_Wq, np.float32),
                       np.asarray(L0_attn_Wkv, np.float32),
                       np.asarray(L0_attn_Wo, np.float32),
                       np.asarray(L0_ffn_U1, np.float32),
                       np.asarray(L0_ffn_V1, np.float32),
                       np.asarray(L0_ffn_U2, np.float32),
                       np.asarray(L0_ffn_V2, np.float32),
                       np.asarray(L0_ffn_b1, np.float32),
                       np.asarray(L0_ffn_b2, np.float32))
    consts = [_consts_for_core(j) for j in range(4)]
    rows = [_core_token_rows(j) for j in range(4)]

    if "A" not in _CACHE:
        _CACHE["A"] = build_A()
    if "B" not in _CACHE:
        _CACHE["B"] = build_B()

    in_A = []
    for c in range(8):
        b, j = c // 4, c % 4
        xc = np.ascontiguousarray(x[b][rows[j], :].T.reshape(NPT, 128, TOK))
        in_A.append({"x": xc, "u1p": wp["u1p"], "v1u2p": wp["v1u2p"],
                     "v2p": wp["v2p"], "wkvp": wp["wkvp"], "b1p": wp["b1p"],
                     "b2p": wp["b2p"], "ones_bf": consts[j]["ones_bf"],
                     "ones_f32r": consts[j]["ones_f32r"]})
    resA = run_bass_kernel_spmd(_CACHE["A"], in_A, core_ids=list(range(8)))

    in_B = []
    for c in range(8):
        b, j = c // 4, c % 4
        kvg = np.stack([np.asarray(resA.results[b * 4 + r]["kv"]) for r in range(4)])
        cc = consts[j]
        in_B.append({"x1": np.asarray(resA.results[c]["x1"]),
                     "y2": np.asarray(resA.results[c]["y2"]),
                     "kvg": np.ascontiguousarray(kvg),
                     "wqp": wp["wqp"], "wop": wp["wop"], "u1p": wp["u1p"],
                     "v1u2p": wp["v1u2p"], "v2p": wp["v2p"], "b1p": wp["b1p"],
                     "b2p": wp["b2p"], "r0": cc["r0"], "mask1": cc["mask1"],
                     "otab": cc["otab"], "idn_bf": cc["idn_bf"],
                     "idn_f32": cc["idn_f32"], "ones_bf": cc["ones_bf"],
                     "ones_f32r": cc["ones_f32r"]})
    resB = run_bass_kernel_spmd(_CACHE["B"], in_B, core_ids=list(range(8)))

    out = np.empty((NB, S, D), dtype=np.float32)
    for c in range(8):
        b, j = c // 4, c % 4
        oc = np.asarray(resB.results[c]["out"]).reshape(D, TOK)
        out[b][rows[j], :] = oc.T
    return out
